# version 20
# baseline (speedup 1.0000x reference)
import numpy as np
import contextlib
import concourse.bass as bass
import concourse.mybir as mybir
from concourse.bass_utils import run_bass_kernel_spmd

F32 = mybir.dt.float32
BF16 = mybir.dt.bfloat16
FP8 = mybir.dt.float8e4
I32 = mybir.dt.int32
AF = mybir.ActivationFunctionType
ALU = mybir.AluOpType
AX = mybir.AxisListType

SEQ, C, H, D, HID, T = 2048, 1024, 16, 64, 4096, 128
NT = SEQ // T
NSEQ = 2
KC = C // 128
JC = HID // 128
EPS_LN = 1e-5
EPS = 1e-6
NEWT = 7
N_CORES = 8

_TAPS = ()
_MAXPHASE = 99


class _B:
    pass


def _mk(nc, b, es):
    sb = lambda name, shape, dt: es.enter_context(nc.sbuf_tensor(name, shape, dt))
    b.wq_t = [sb(f"wq_t{k}", [128, C], F32) for k in range(KC)]
    b.wp_t = [sb(f"wp_t{k}", [128, C], F32) for k in range(KC)]
    b.w1q = [sb(f"w1q{k}", [128, HID], FP8) for k in range(KC)]
    b.w2q = [sb(f"w2q{j}", [128, C], FP8) for j in range(JC)]
    b.arena = [sb(f"arena{k}", [128, 2048], F32) for k in range(KC)]
    b.idf = sb("idf_s", [128, 128], F32)
    b.idb = sb("idb_s", [128, 128], BF16)
    b.ones_col = sb("ones_col", [128, 1], F32)
    b.ones_row = sb("ones_row", [1, 128], F32)
    b.mask2 = sb("mask2", [2, 128], F32)
    b.onesh = sb("onesh", [128, 2], F32)
    b.dwc_w0 = sb("dwc_w0", [128, 8], F32)
    b.dwc_w1p = sb("dwc_w1p", [128, 8], F32)
    b.dwc_w2 = sb("dwc_w2", [128, 8], F32)
    b.bdwc = sb("bdwc", [128, 8], F32)
    b.cw0 = sb("cw0", [128, 32], F32)
    b.cw1 = sb("cw1", [128, 32], F32)
    b.cw2 = sb("cw2", [128, 32], F32)
    b.cbf = sb("cbf", [128, 32], F32)
    b.bproj_bc = sb("bproj_bc", [128, C], F32)
    b.kv_sb = [sb(f"kv_sb{s}", [128, 512], F32) for s in range(NSEQ)]
    b.zmask = [sb(f"zmask{s}", [128, 16], F32) for s in range(NSEQ)]
    b.wsc = sb("wsc", [128, 4], F32)
    b.asum = sb("asum", [128, 64], F32)
    b.sml = sb("sml", [1, 8], F32)
    b.tmp = sb("tmp", [128, 32], F32)
    b.zd = sb("zd", [2, 384], F32)
    b.mixa = sb("mixa", [128, 128], F32)[:, :]
    b.mixb = sb("mixb", [128, 128], F32)[:, :]
    b.mixc = sb("mixc", [128, 128], F32)[:, :]
    b.mixd = sb("mixd", [128, 128], F32)[:, :]

    A = b.arena
    b.bproj_r = A[7][0:1, 0:1024]
    b.dwc = A[6][:, 0:24]
    b.cwf = A[6][:, 24:120]
    b.outw = [A[0][:, 0:1040], A[1][:, 0:1040]]
    b.s_loc = []
    for j in range(JC):
        if j < 28:
            sl, off = j // 7, (1040 if j < 14 else 1024) + (j % 7) * 130
        else:
            sl, off = 4 + (j - 28) // 2, 1024 + ((j - 28) % 2) * 130
        b.s_loc.append(A[sl][:, off:off + 130])
    b.x1r = [A[2][:, 0:1024], A[3][:, 0:1024]]
    b.xr = [A[4][:, 0:1024], A[5][:, 0:1024]]
    b.xqw = [A[4][:, 1284:1812].bitcast(BF16),
             A[5][:, 1284:1812].bitcast(BF16)]
    b.xnfm = A[6][:, 0:1024]
    b.q3 = A[7][:, 1024:2048]
    b.i32b = A[6][:, 1024:1536].bitcast(I32)
    b.xqtm = A[6][:, 1536:1792].bitcast(BF16)
    b.M1 = A[6][:, 1792:1920]
    b.s2b = A[6][:, 1920:2048]
    b.scratch = A[7][:, 1024:2048]
    b.xntm = b.scratch
    b.sq = [A[7][:, 0:256].bitcast(BF16), A[7][:, 256:512].bitcast(BF16)]
    b.cbc = A[7][:, 512:642]
    b.cw_win = A[7][0:1, 642:906]
    b.x2buf = b.scratch

    b.PS = [es.enter_context(nc.psum_tensor(f"ps{i}", [128, 512], F32))
            for i in range(8)]

    b.s_ld = es.enter_context(nc.semaphore("s_ld"))
    b.s_ldb = es.enter_context(nc.semaphore("s_ldb"))
    b.s_st = es.enter_context(nc.semaphore("s_st"))
    b.s_pe = es.enter_context(nc.semaphore("s_pe"))
    b.s_dve = es.enter_context(nc.semaphore("s_dve"))
    b.s_act = es.enter_context(nc.semaphore("s_act"))
    b.s_pool = es.enter_context(nc.semaphore("s_pool"))
    b.n = {"ld": 0, "ldb": 0, "st": 0, "pe": 0, "dve": 0, "act": 0, "pool": 0}
    b.sem = {"ld": b.s_ld, "ldb": b.s_ldb, "st": b.s_st, "pe": b.s_pe,
             "dve": b.s_dve, "act": b.s_act, "pool": b.s_pool}


class _Ch:
    _PASS = ("wait_ge", "nop", "sem_inc", "sem_clear", "drain")

    def __init__(self, b, eng, key):
        self._b, self._e, self._k = b, eng, key

    def __getattr__(self, name):
        f = getattr(self._e, name)
        if name in _Ch._PASS or not callable(f):
            return f
        b, e, k = self._b, self._e, self._k

        def w(*a, **kw):
            if b.n[k] > 0:
                e.wait_ge(b.sem[k], b.n[k])
            inst = f(*a, **kw)
            inst.then_inc(b.sem[k], 1)
            b.n[k] += 1
            return inst
        return w


def _i(b, inst, eng, n=1):
    if eng in ("dve", "act", "pool"):
        return b.n[eng]
    inst.then_inc(b.sem[eng], n)
    b.n[eng] += n
    return b.n[eng]


def _pload(b, sy, out, in_, parity):
    eng = "ld" if parity == 0 else "ldb"
    sy.wait_ge(b.sem[eng], b.n[eng])
    _i(b, sy.dma_start(out=out, in_=in_), eng, 16)
    return (b.sem[eng], b.n[eng])


def _round_robust(v, b, t_ap, i32_ap, r_ap, g_ap, out_ap):
    v.tensor_copy(i32_ap, t_ap)
    v.tensor_tensor(r_ap, t_ap, i32_ap, ALU.subtract)
    v.tensor_single_scalar(g_ap, r_ap, 0.5, ALU.is_ge)
    v.scalar_tensor_tensor(g_ap, r_ap, -0.5, g_ap, ALU.is_le,
                           ALU.subtract)
    return v.tensor_tensor(out_ap, i32_ap, g_ap, ALU.subtract)


def _tap(nc, blk, b, name, src, waits=()):
    if name not in b.tapd:
        return
    @blk.gpsimd
    def _(g):
        for sem, val in waits:
            g.wait_ge(sem, val)
        g.wait_ge(b.s_st, b.n["st"])
        _i(b, g.dma_start(out=b.tapd[name], in_=src), "st", 16)
    nst = b.n["st"]

    @blk.vector
    def _(v):
        v.wait_ge(b.s_st, nst)

    @blk.scalar
    def _(sc):
        sc.wait_ge(b.s_st, nst)


def _phase_gate(nc, blk, b):
    @blk.sync
    def _(sy):
        sy.wait_ge(b.s_ld, b.n["ld"])
        sy.wait_ge(b.s_ldb, b.n["ldb"])


def _newton_rsqrt(v, out_c, w_c, t1, t2):
    v.tensor_scalar(out_c, w_c, -0.5, 1.5, ALU.mult, ALU.add)
    li = v.tensor_scalar(out_c, out_c, 0.18, 1.4, ALU.max, ALU.min)
    for _ in range(NEWT):
        v.tensor_tensor(t1, out_c, out_c, ALU.mult)
        v.tensor_tensor(t2, t1, w_c, ALU.mult)
        v.tensor_scalar(t2, t2, -0.5, 1.5, ALU.mult, ALU.add)
        li = v.tensor_tensor(out_c, out_c, t2, ALU.mult)
    return li


def build(taps=(), maxphase=99):
    nc = bass.Bass("TRN2", target_bir_lowering=False, detect_race_conditions=True)
    b = _B()
    b.taps = set(taps)

    b.x_d = nc.dram_tensor("x", [NSEQ * SEQ, C], F32, kind="ExternalInput").ap()
    b.wqkv_d = nc.dram_tensor("w_qkv", [3 * C, C], F32, kind="ExternalInput").ap()
    b.wproj_d = nc.dram_tensor("w_proj", [C, C], F32, kind="ExternalInput").ap()
    b.bproj_d = nc.dram_tensor("b_proj", [1, C], F32, kind="ExternalInput").ap()
    b.wdwc_d = nc.dram_tensor("w_dwc", [C, 3], F32, kind="ExternalInput").ap()
    b.bdwc_d = nc.dram_tensor("b_dwc", [C, 1], F32, kind="ExternalInput").ap()
    b.w1_d = nc.dram_tensor("w1", [HID, C], F32, kind="ExternalInput").ap()
    b.cw_d = nc.dram_tensor("conv_w", [HID, 3], F32, kind="ExternalInput").ap()
    b.cb_d = nc.dram_tensor("conv_b", [HID, 1], F32, kind="ExternalInput").ap()
    b.w2_d = nc.dram_tensor("w2", [C, HID], F32, kind="ExternalInput").ap()
    b.idf_d = nc.dram_tensor("idf", [128, 128], F32, kind="ExternalInput").ap()
    b.idb_d = nc.dram_tensor("idb", [128, 128], BF16, kind="ExternalInput").ap()
    b.cmask_d = nc.dram_tensor("cmask", [2, 128], F32, kind="ExternalInput").ap()
    b.y_d = nc.dram_tensor("y", [NSEQ * SEQ, C], F32, kind="ExternalOutput").ap()

    b.tapd = {}
    tapshapes = {"kv0": ([128, 512], F32), "zmask0": ([128, 16], F32),
                 "xnfm": ([128, 1024], F32), "q3": ([128, 1024], F32),
                 "outw": ([128, 1040], F32), "x1": ([128, 1024], F32),
                 "xq": ([128, 1056], BF16), "sfull": ([128, 520], F32),
                 "m1": ([128, 128], F32), "wsc": ([128, 4], F32),
                 "cbc": ([128, 130], F32),
                 "w1q": ([128, 4096], F32), "w2q": ([128, 1024], F32)}
    for name in b.taps:
        shape, dt = tapshapes[name]
        b.tapd[name] = nc.dram_tensor("tap_" + name, shape, dt,
                                      kind="ExternalOutput").ap()

    with contextlib.ExitStack() as es:
        _mk(nc, b, es)
        _emit_P0(nc, b)
        if maxphase >= 1:
            _emit_wqkvT(nc, b)
        if maxphase >= 2:
            _emit_ffn_quant(nc, b)
        if maxphase >= 3:
            _emit_A(nc, b, 0)
        if maxphase >= 4:
            _emit_A(nc, b, 1)
        if maxphase >= 5:
            _emit_wprojT(nc, b)
        if maxphase >= 6:
            _emit_BC(nc, b, 0)
        if maxphase >= 7:
            _emit_BC(nc, b, 1)
        _emit_finish(nc, b)
    return nc


def _emit_P0(nc, b):
    with nc.Block() as blk:
        @blk.sync
        def _(sy):
            dma = lambda o, i_: _i(b, sy.dma_start(out=o, in_=i_), "ld", 16)
            dma(b.idf[:], b.idf_d)
            dma(b.idb[:], b.idb_d)
            dma(b.mask2[:], b.cmask_d)
            for c in range(8):
                dma(b.dwc[:, c * 3:(c + 1) * 3], b.wdwc_d[c * 128:(c + 1) * 128, :])
                dma(b.bdwc[:, c:c + 1], b.bdwc_d[c * 128:(c + 1) * 128, :])
            for j in range(JC):
                dma(b.cwf[:, j * 3:(j + 1) * 3], b.cw_d[j * 128:(j + 1) * 128, :])
                dma(b.cbf[:, j:j + 1], b.cb_d[j * 128:(j + 1) * 128, :])
            dma(b.bproj_r[:], b.bproj_d)

        @blk.vector
        def _(v):
            v = _Ch(b, v, "dve")
            v.memset(b.ones_col[:], 1.0)
            v.memset(b.ones_row[:], 1.0)
            v.memset(b.onesh[:], 0.0)
            v.memset(b.onesh[0:64, 0:1], 1.0)
            v.memset(b.onesh[64:128, 1:2], 1.0)
            v.wait_ge(b.s_ld, b.n["ld"])
            dv = b.dwc.rearrange("p (c j) -> p c j", j=3)
            v.tensor_copy(b.dwc_w0[:], dv[:, :, 0])
            v.tensor_scalar_add(b.dwc_w1p[:], dv[:, :, 1], 1.0)
            v.tensor_copy(b.dwc_w2[:], dv[:, :, 2])
            cv = b.cwf.rearrange("p (c j) -> p c j", j=3)
            v.tensor_copy(b.cw0[:], cv[:, :, 0])
            v.tensor_copy(b.cw1[:], cv[:, :, 1])
            _i(b, v.tensor_copy(b.cw2[:], cv[:, :, 2]), "dve")
        ndve0 = b.n["dve"]

        @blk.tensor
        def _(t, ndve0=ndve0):
            t.wait_ge(b.s_ld, b.n["ld"])
            t.wait_ge(b.s_dve, ndve0)
            t.matmul(b.PS[3][:], b.ones_row[:], b.bproj_r[0:1, 0:512],
                     start=True, stop=True)
            _i(b, t.matmul(b.PS[4][:], b.ones_row[:], b.bproj_r[0:1, 512:1024],
                           start=True, stop=True), "pe")

        @blk.scalar
        def _(s):
            s = _Ch(b, s, "act")
            s.wait_ge(b.s_pe, b.n["pe"])
            s.copy(b.bproj_bc[:, 0:512], b.PS[3][:])
            _i(b, s.copy(b.bproj_bc[:, 512:1024], b.PS[4][:]), "act")


def _emit_wqkvT(nc, b):
    PS = b.PS
    with nc.Block() as blk:
        _phase_gate(nc, blk, b)
        evq = {}
        for i in range(24):
            @blk.sync
            def _(sy, i=i):
                if i >= 2:
                    sy.wait_ge(b.s_act, evq[i - 2])
                b.lw = _pload(b, sy, b.wp_t[i % 2][:],
                              b.wqkv_d[i * 128:(i + 1) * 128, :], i % 2)
            lw = b.lw

            @blk.tensor
            def _(t, i=i, lw=lw):
                t.wait_ge(*lw)
                if i >= 1:
                    t.wait_ge(b.s_act, evq[i - 1])
                for k in range(KC):
                    p = PS[k // 4][:, (k % 4) * 128:(k % 4 + 1) * 128]
                    _li = t.transpose(p, b.wp_t[i % 2][:, k * 128:(k + 1) * 128], b.idf[:])
                _i(b, _li, "pe")
            npe = b.n["pe"]

            @blk.scalar
            def _(s, i=i, npe=npe):
                s = _Ch(b, s, "act")
                s.wait_ge(b.s_pe, npe)
                for k in range(KC):
                    p = PS[k // 4][:, (k % 4) * 128:(k % 4 + 1) * 128]
                    dest = (b.wq_t[k][:, i * 128:(i + 1) * 128] if i < 8 else
                            b.arena[k][:, (i - 8) * 128:(i - 7) * 128])
                    _li = s.copy(dest, p)
                _i(b, _li, "act")
            evq[i] = b.n["act"]


def _emit_wprojT(nc, b):
    PS = b.PS
    with nc.Block() as blk:
        _phase_gate(nc, blk, b)
        evq = {}
        for i in range(8):
            stg = b.arena[6][:, (i % 2) * 1024:(i % 2 + 1) * 1024]
            @blk.sync
            def _(sy, i=i, stg=stg):
                if i >= 2:
                    sy.wait_ge(b.s_act, evq[i - 2])
                b.lw = _pload(b, sy, stg, b.wproj_d[i * 128:(i + 1) * 128, :], i % 2)
            lw = b.lw

            @blk.tensor
            def _(t, i=i, stg=stg, lw=lw):
                t.wait_ge(*lw)
                if i >= 1:
                    t.wait_ge(b.s_act, evq[i - 1])
                for k in range(KC):
                    p = PS[k // 4][:, (k % 4) * 128:(k % 4 + 1) * 128]
                    _li = t.transpose(p, stg[:, k * 128:(k + 1) * 128], b.idf[:])
                _i(b, _li, "pe")
            npe = b.n["pe"]

            @blk.scalar
            def _(s, i=i, npe=npe):
                s = _Ch(b, s, "act")
                s.wait_ge(b.s_pe, npe)
                for k in range(KC):
                    p = PS[k // 4][:, (k % 4) * 128:(k % 4 + 1) * 128]
                    _li = s.copy(b.wp_t[k][:, i * 128:(i + 1) * 128], p)
                _i(b, _li, "act")
            evq[i] = b.n["act"]


def _w_src(b, i):
    if i < 32:
        return b.w1_d[i * 128:(i + 1) * 128, :]
    j = i - 32
    oc, q = j % 8, j // 8
    return b.w2_d[oc * 128:(oc + 1) * 128, q * 1024:(q + 1) * 1024]


def _emit_ffn_quant(nc, b):
    PS = b.PS
    with nc.Block() as blk:
        _phase_gate(nc, blk, b)
        evq = {}
        for i in range(64):
            @blk.sync
            def _(sy, i=i):
                if i >= 2:
                    sy.wait_ge(b.s_act, evq[i - 2])
                b.lw = _pload(b, sy, b.wp_t[i % 2][:], _w_src(b, i), i % 2)
            lw = b.lw

            @blk.scalar
            def _(s, i=i, lw=lw):
                s = _Ch(b, s, "act")
                s.wait_ge(*lw)
                _i(b, s.activation(b.wp_t[2][:], b.wp_t[i % 2][:], AF.Abs,
                                   accum_out=b.asum[:, i:i + 1]), "act")
            evq[i] = b.n["act"]

        @blk.vector
        def _(v):
            v = _Ch(b, v, "dve")
            v.wait_ge(b.s_act, b.n["act"])
            v.reduce_sum(b.tmp[:, 0:1], b.asum[:, 0:32], axis=AX.X)
            _i(b, v.reduce_sum(b.tmp[:, 1:2], b.asum[:, 32:64], axis=AX.X), "dve")
        nd = b.n["dve"]

        @blk.tensor
        def _(t, nd=nd):
            t.wait_ge(b.s_dve, nd)
            t.matmul(PS[2][0:1, 0:1], b.tmp[:, 0:1], b.ones_col[:], start=True, stop=True)
            _i(b, t.matmul(PS[2][0:1, 1:2], b.tmp[:, 1:2], b.ones_col[:],
                           start=True, stop=True), "pe")
        np1 = b.n["pe"]

        @blk.vector
        def _(v, np1=np1):
            v = _Ch(b, v, "dve")
            v.wait_ge(b.s_pe, np1)
            v.tensor_scalar(b.sml[0:1, 0:2], PS[2][0:1, 0:2], 1.0 / (HID * C), 1e-5,
                            ALU.mult, ALU.max)
            _i(b, v.reciprocal(b.sml[0:1, 2:4], b.sml[0:1, 0:2]), "dve")
        nd2 = b.n["dve"]

        @blk.tensor
        def _(t, nd2=nd2):
            t.wait_ge(b.s_dve, nd2)
            t.matmul(PS[2][:, 8:9], b.ones_row[:], b.sml[0:1, 2:3], start=True, stop=True)
            t.matmul(PS[2][:, 9:10], b.ones_row[:], b.sml[0:1, 3:4], start=True, stop=True)
            t.matmul(PS[2][:, 10:11], b.ones_row[:], b.sml[0:1, 0:1], start=True, stop=True)
            _i(b, t.matmul(PS[2][:, 11:12], b.ones_row[:], b.sml[0:1, 1:2],
                           start=True, stop=True), "pe")
        np2 = b.n["pe"]

        @blk.vector
        def _(v, np2=np2):
            v = _Ch(b, v, "dve")
            v.wait_ge(b.s_pe, np2)
            _i(b, v.tensor_copy(b.wsc[:, 0:4], PS[2][:, 8:12]), "dve")

        if "wsc" in b.tapd:
            ndv = b.n["dve"]
            @blk.gpsimd
            def _(g, ndv=ndv):
                g.wait_ge(b.s_dve, ndv)
                g.wait_ge(b.s_st, b.n["st"])
                _i(b, g.dma_start(out=b.tapd["wsc"], in_=b.wsc[:]), "st", 16)

    i32v = b.wp_t[3][:, :].bitcast(I32)
    bfv = b.wp_t[4][:, :].bitcast(BF16)[:, 0:1024]
    with nc.Block() as blk:
        _phase_gate(nc, blk, b)
        evq = {}
        for i in range(64):
            @blk.sync
            def _(sy, i=i):
                if i >= 2:
                    sy.wait_ge(b.s_act, evq[i - 2])
                b.lw = _pload(b, sy, b.wp_t[i % 2][:], _w_src(b, i), i % 2)
            lw = b.lw

            @blk.vector
            def _(v, i=i, lw=lw):
                v = _Ch(b, v, "dve")
                v.wait_ge(*lw)
                if i >= 1:
                    v.wait_ge(b.s_pe, evp)
                wscol = b.wsc[:, 0:1] if i < 32 else b.wsc[:, 1:2]
                w5 = b.wp_t[5][:, :]
                v.tensor_scalar(w5, b.wp_t[i % 2][:], wscol, -1.499,
                                ALU.mult, ALU.max)
                v.tensor_scalar_min(w5, w5, 1.499)
                _i(b, _round_robust(v, b, w5, i32v, b.wp_t[6][:, :],
                                    b.wp_t[7][:, :], bfv), "dve")
            ndv = b.n["dve"]

            @blk.tensor
            def _(t, i=i, ndv=ndv):
                t.wait_ge(b.s_dve, ndv)
                if i >= 1:
                    t.wait_ge(b.s_act, evq[i - 1])
                for k in range(KC):
                    p = PS[5][:, k * 64:(k + 1) * 64].bitcast(BF16)
                    _li = t.transpose(p, bfv[:, k * 128:(k + 1) * 128], b.idb[:])
                _i(b, _li, "pe")
            evp = b.n["pe"]

            @blk.scalar
            def _(s, i=i, evp=evp):
                s = _Ch(b, s, "act")
                s.wait_ge(b.s_pe, evp)
                for k in range(KC):
                    p = PS[5][:, k * 64:(k + 1) * 64].bitcast(BF16)
                    if i < 32:
                        dest = b.w1q[k][:, i * 128:(i + 1) * 128]
                    else:
                        j = i - 32
                        oc, q = j % 8, j // 8
                        dest = b.w2q[q * 8 + k][:, oc * 128:(oc + 1) * 128]
                    _li = s.copy(dest, p)
                _i(b, _li, "act")
            evq[i] = b.n["act"]

        if "w1q" in b.tapd:
            na = b.n["act"]
            @blk.gpsimd
            def _(g, na=na):
                g.wait_ge(b.s_act, na)
                g.wait_ge(b.s_st, b.n["st"])
                _i(b, g.dma_start(out=b.tapd["w1q"], in_=b.w1q[0][:, :]), "st", 16)
        if "w2q" in b.tapd:
            na = b.n["act"]
            @blk.gpsimd
            def _(g, na=na):
                g.wait_ge(b.s_act, na)
                g.wait_ge(b.s_st, b.n["st"])
                _i(b, g.dma_start(out=b.tapd["w2q"], in_=b.w2q[0][:, :]), "st", 16)


def _ln_small_ops(b, v, mcol, rcol):
    tm = b.tmp
    v.tensor_scalar_mul(tm[:, mcol:mcol + 1], tm[:, mcol:mcol + 1], 1.0 / C)
    v.tensor_scalar_mul(tm[:, 9:10], tm[:, 8:9], 1.0 / C)
    v.tensor_tensor(tm[:, 10:11], tm[:, mcol:mcol + 1], tm[:, mcol:mcol + 1], ALU.mult)
    v.scalar_tensor_tensor(tm[:, 11:12], tm[:, 10:11], -1.0, tm[:, 9:10],
                           ALU.mult, ALU.add)
    v.tensor_scalar_add(tm[:, 11:12], tm[:, 11:12], EPS_LN)
    _newton_rsqrt(v, tm[:, rcol:rcol + 1], tm[:, 11:12], tm[:, 12:13], tm[:, 13:14])
    return v.tensor_scalar(tm[:, mcol + 1:mcol + 2], tm[:, mcol:mcol + 1],
                           tm[:, rcol:rcol + 1], -1.0, ALU.mult, ALU.mult)


def _emit_A(nc, b, s):
    PS = b.PS
    W = b.wp_t
    xr = [W[0][:, :], W[1][:, :]]
    xntm, xnfm = W[2][:, :], W[3][:, :]
    ak, khat, vsb = W[4][:, :], W[5][:, :], W[6][:, :]
    lns = W[7][:, :]

    with nc.Block() as blk:
        _phase_gate(nc, blk, b)
        done = {}
        for t in range(NT):
            @blk.sync
            def _(sy, t=t):
                if t >= 2:
                    sy.wait_ge(b.s_dve, done[t - 2][0])
                    sy.wait_ge(b.s_act, done[t - 2][1])
                b.lw = _pload(b, sy, xr[t % 2],
                              b.x_d[s * SEQ + t * T:s * SEQ + (t + 1) * T, :], t % 2)
            lw = b.lw

            @blk.vector
            def _(v, t=t, lw=lw):
                v = _Ch(b, v, "dve")
                v.wait_ge(*lw)
                if t >= 1:
                    v.wait_ge(b.s_act, b.a_prev_xn)
                _i(b, v.reduce_sum(b.tmp[:, 0:1], xr[t % 2], axis=AX.X), "dve")
            nsum = b.n["dve"]

            @blk.scalar
            def _(sc, t=t, lw=lw):
                sc = _Ch(b, sc, "act")
                sc.wait_ge(*lw)
                if t >= 1:
                    sc.wait_ge(b.s_pe, b.a_prev_tr)
                _i(b, sc.activation(lns, xr[t % 2], AF.Square,
                                    accum_out=b.tmp[:, 8:9]), "act")
            nsq = b.n["act"]

            @blk.vector
            def _(v, nsq=nsq):
                v = _Ch(b, v, "dve")
                v.wait_ge(b.s_act, nsq)
                _li = _ln_small_ops(b, v, 0, 2)
                _i(b, _li, "dve")
            nln = b.n["dve"]

            @blk.scalar
            def _(sc, t=t, nln=nln):
                sc = _Ch(b, sc, "act")
                sc.wait_ge(b.s_dve, nln)
                if t >= 1:
                    sc.wait_ge(b.s_pe, b.a_prev_tr)
                _i(b, sc.activation(xntm, xr[t % 2], AF.Identity,
                                    bias=b.tmp[:, 1:2], scale=b.tmp[:, 2:3]), "act")
            nxn = b.n["act"]
            b.a_prev_xn = nxn

            @blk.tensor
            def _(tt, t=t, nxn=nxn):
                tt.wait_ge(b.s_act, nxn)
                if t >= 1:
                    tt.wait_ge(b.s_act, b.a_prev_fm)
                for k in range(KC):
                    p = PS[k // 4][:, (k % 4) * 128:(k % 4 + 1) * 128]
                    _li = tt.transpose(p, xntm[:, k * 128:(k + 1) * 128], b.idf[:])
                _i(b, _li, "pe")
            ntr = b.n["pe"]
            b.a_prev_tr = ntr

            @blk.scalar
            def _(sc, ntr=ntr):
                sc = _Ch(b, sc, "act")
                sc.wait_ge(b.s_pe, ntr)
                sc.copy(xnfm[:, 0:512], PS[0][:])
                _li = sc.copy(xnfm[:, 512:1024], PS[1][:])
                _i(b, _li, "act")
            nfm = b.n["act"]
            b.a_prev_fm = nfm

            @blk.tensor
            def _(tt, t=t, nfm=nfm):
                tt.wait_ge(b.s_act, nfm)
                if t >= 1:
                    tt.wait_ge(b.s_dve, b.a_prev_ak)
                    tt.wait_ge(b.s_act, b.a_prev_vev)
                for f in range(4):
                    for k in range(KC):
                        _li = tt.matmul(PS[2 + f][:], xnfm[:, k * 128:(k + 1) * 128],
                                  b.arena[k][:, f * 512:(f + 1) * 512],
                                  start=(k == 0), stop=(k == KC - 1))
                _i(b, _li, "pe")
            nkv = b.n["pe"]

            @blk.vector
            def _(v, nkv=nkv, t=t):
                v = _Ch(b, v, "dve")
                v.wait_ge(b.s_pe, nkv)
                if t >= 1:
                    v.wait_ge(b.s_pe, b.a_prev_acc)
                v.tensor_scalar(ak[:, 0:512], PS[2][:], 0.0, EPS, ALU.max, ALU.add)
                v.tensor_scalar(ak[:, 512:1024], PS[3][:], 0.0, EPS, ALU.max, ALU.add)
                v.tensor_tensor(khat, ak, ak, ALU.mult)
                v.tensor_tensor(khat, khat, ak, ALU.mult)
                akv = khat.rearrange("p (h d) -> p h d", d=D)
                v.reduce_sum(b.tmp[:, 16:32], akv, axis=AX.X)
                v.reciprocal(b.tmp[:, 16:32], b.tmp[:, 16:32])
                rb = b.tmp[:, 16:32].unsqueeze(2).broadcast_to([128, 16, D])
                _i(b, v.tensor_tensor(akv, akv, rb, ALU.mult), "dve")
            nak = b.n["dve"]
            b.a_prev_ak = nak

            @blk.scalar
            def _(sc, nkv=nkv, t=t):
                sc = _Ch(b, sc, "act")
                sc.wait_ge(b.s_pe, nkv)
                if t >= 1:
                    sc.wait_ge(b.s_pe, b.a_prev_acc)
                sc.copy(vsb[:, 0:512], PS[4][:])
                _i(b, sc.copy(vsb[:, 512:1024], PS[5][:]), "act")
            nvev = b.n["act"]
            b.a_prev_vev = nvev

            @blk.tensor
            def _(tt, t=t, nak=nak, nvev=nvev):
                tt.wait_ge(b.s_dve, nak)
                tt.wait_ge(b.s_act, nvev)
                for h in range(H):
                    c, r = h // 2, (h % 2) * 64
                    tt.matmul(PS[6][r:r + 64, c * 64:(c + 1) * 64],
                              khat[:, h * D:(h + 1) * D], vsb[:, h * D:(h + 1) * D],
                              start=(t == 0 and h < 2),
                              stop=(t == NT - 1 and h >= H - 2),
                              tile_position=(0, r), skip_group_check=True)
                for h in range(H):
                    c, r = h // 2, (h % 2) * 64
                    _li = tt.matmul(PS[7][r:r + 64, c:c + 1],
                              khat[:, h * D:(h + 1) * D], b.ones_col[:],
                              start=(t == 0 and h < 2),
                              stop=(t == NT - 1 and h >= H - 2),
                              tile_position=(0, r), skip_group_check=True)
                _i(b, _li, "pe")
            b.a_prev_acc = b.n["pe"]
            done[t] = (nak, nvev)

        nacc = b.n["pe"]
        @blk.vector
        def _(v, nacc=nacc, s=s):
            v = _Ch(b, v, "dve")
            v.wait_ge(b.s_pe, nacc)
            v.tensor_copy(b.kv_sb[s][:], PS[6][:])
            v.memset(b.zmask[s][:], 0.0)
            for h in range(H):
                c, r = h // 2, (h % 2) * 64
                _li = v.tensor_copy(b.zmask[s][r:r + 64, h:h + 1], PS[7][r:r + 64, c:c + 1])
            _i(b, _li, "dve")

        if s == 0 and ("kv0" in b.tapd or "zmask0" in b.tapd):
            ndv = b.n["dve"]
            @blk.gpsimd
            def _(g, ndv=ndv):
                g.wait_ge(b.s_dve, ndv)
                g.wait_ge(b.s_st, b.n["st"])
                if "kv0" in b.tapd:
                    _i(b, g.dma_start(out=b.tapd["kv0"], in_=b.kv_sb[0][:]), "st", 16)
                if "zmask0" in b.tapd:
                    _i(b, g.dma_start(out=b.tapd["zmask0"], in_=b.zmask[0][:]), "st", 16)


_QOC = [0, 4, 1, 5, 2, 6, 3, 7]
_BC_STEPS = {0: NT + 2, 1: NT + 2}


def _emit_BC(nc, b, s):
    nsteps = _BC_STEPS.get(s, NT + 2)
    PS = b.PS
    tapt = 3
    dotap = (s == 0)

    with nc.Block() as blk:
        _phase_gate(nc, blk, b)
        if s == 0:
            @blk.vector
            def _(v):
                v = _Ch(b, v, "dve")
                for w in range(2):
                    pad = b.xqw[w].rearrange("p (c w) -> p c w", w=132)[:, :, 130:132]
                    v.memset(pad, 0.0)
        b.bc_scr_free = b.n["st"]
        b.bc_x_used = {}
        b.bc_x1_used = {}
        b.bc_ow_ok = {}
        b.bc_xw_ok = {}
        b.bc_s4 = {}
        b.bc_s6 = {}

        for step in range(nsteps):
            t = step
            if t < NT:
                _bc_S1(nc, blk, b, s, t)
                _bc_S2(nc, blk, b, s, t, dotap and t == tapt)
                _bc_S3(nc, blk, b, s, t, dotap and t == tapt)
            if 1 <= step <= NT:
                _bc_S4(nc, blk, b, s, step - 1, dotap and step - 1 == tapt)
                _bc_S5(nc, blk, b, s, step - 1, dotap and step - 1 == tapt)
            if 2 <= step <= NT + 1:
                _bc_S6(nc, blk, b, s, step - 2, dotap and step - 2 == tapt)
                _bc_S7(nc, blk, b, s, step - 2)


def _ow3(b, t):
    return b.outw[t % 2].rearrange("p (c w) -> p c w", w=130)


def _xw3(b, t):
    return b.xqw[t % 2].rearrange("p (c w) -> p c w", w=132)


def _cwin(b, t):
    return b.cw_win[0:1, (t % 2) * 132:(t % 2) * 132 + 132]


def _bc_S1(nc, blk, b, s, t):
    @blk.sync
    def _(sy, t=t):
        if t >= 2:
            for sem, val in b.bc_x_used.get(t - 2, ()):
                sy.wait_ge(sem, val)
        b.bc_lw = _pload(b, sy, b.xr[t % 2],
                         b.x_d[s * SEQ + t * T:s * SEQ + (t + 1) * T, :], t % 2)


def _bc_S2(nc, blk, b, s, t, tap):
    PS = b.PS
    lw = b.bc_lw

    act0 = b.n["act"]
    pe0 = b.n["pe"]

    @blk.vector
    def _(v, t=t, lw=lw, act0=act0):
        v = _Ch(b, v, "dve")
        v.wait_ge(*lw)
        v.reduce_sum(b.tmp[:, 0:1], b.xr[t % 2], axis=AX.X)
        v.wait_ge(b.s_act, act0 + 1)
        _li = _ln_small_ops(b, v, 0, 2)
        _i(b, _li, "dve")
    nsum = b.n["dve"]
    nln = b.n["dve"]
    dve0q = b.n["dve"]

    @blk.tensor
    def _(tt, act0=act0):
        tt.wait_ge(b.s_act, act0 + 2)
        for k in range(KC):
            p = PS[k // 4][:, (k % 4) * 128:(k % 4 + 1) * 128]
            _li = tt.transpose(p, b.xntm[:, k * 128:(k + 1) * 128], b.idf[:])
        _i(b, _li, "pe")
        tt.wait_ge(b.s_act, act0 + 4)
        for idx, oc in enumerate(_QOC):
            if idx >= 2:
                tt.wait_ge(b.s_dve, dve0q + 3 * (idx - 1))
            p = PS[oc // 4][:, (oc % 4) * 128:(oc % 4 + 1) * 128]
            for k in range(KC):
                _li = tt.matmul(p, b.wq_t[k][:, oc * 128:(oc + 1) * 128],
                          b.xnfm[:, k * 128:(k + 1) * 128],
                          start=(k == 0), stop=(k == KC - 1))
            _i(b, _li, "pe")
    assert b.n["pe"] == pe0 + 1 + KC
    ntr = pe0 + 1
    nq = b.n["pe"]

    @blk.scalar
    def _(sc, t=t, lw=lw, nln=nln, ntr=ntr):
        sc = _Ch(b, sc, "act")
        sc.wait_ge(*lw)
        sc.wait_ge(b.s_st, b.bc_scr_free)
        sc.wait_ge(b.s_pe, getattr(b, "bc_zmm", 0))
        sc.activation(b.scratch, b.xr[t % 2], AF.Square,
                      accum_out=b.tmp[:, 8:9])
        sc.wait_ge(b.s_dve, nln)
        sc.activation(b.xntm, b.xr[t % 2], AF.Identity,
                      bias=b.tmp[:, 1:2], scale=b.tmp[:, 2:3])
        sc.wait_ge(b.s_pe, ntr)
        sc.copy(b.xnfm[:, 0:512], PS[0][:])
        _li = sc.copy(b.xnfm[:, 512:1024], PS[1][:])
        _i(b, _li, "act")
    assert b.n["act"] == act0 + 4
    nxn = act0 + 2
    nfm = b.n["act"]
    if tap:
        _tap(nc, blk, b, "xnfm", b.xnfm, [(b.s_act, nfm)])

    b.bc_q3c = {}
    @blk.vector
    def _(v, ntr=ntr):
        v = _Ch(b, v, "dve")
        for idx, oc in enumerate(_QOC):
            v.wait_ge(b.s_pe, ntr + 1 + idx)
            p = PS[oc // 4][:, (oc % 4) * 128:(oc % 4 + 1) * 128]
            qc = b.q3[:, oc * 128:(oc + 1) * 128]
            v.tensor_scalar(qc, p, 0.0, EPS, ALU.max, ALU.add)
            v.tensor_tensor(b.mixc, qc, qc, ALU.mult)
            v.tensor_tensor(qc, b.mixc, qc, ALU.mult)
            b.bc_q3c[oc] = b.n["dve"]
    assert b.n["dve"] == dve0q + 24
    b.bc_q3 = b.n["dve"]
    if tap:
        _tap(nc, blk, b, "q3", b.q3, [(b.s_dve, b.bc_q3)])

    b.bc_x_used.setdefault(t, []).append((b.s_dve, nsum))
    b.bc_x_used[t].append((b.s_act, nxn))


def _bc_S3(nc, blk, b, s, t, tap):
    PS = b.PS
    nq3 = b.bc_q3

    @blk.tensor
    def _(tt, t=t, s=s):
        for c in range(KC):
            tt.wait_ge(b.s_dve, b.bc_q3c[c])
            p = PS[c // 4][:, (c % 4) * 128:(c % 4 + 1) * 128]
            q3b = b.q3[:, c * 128:(c + 1) * 128]
            kvc = b.kv_sb[s][:, c * 64:(c + 1) * 64]
            tt.matmul(p[0:64, :], kvc[0:64, :], q3b[0:64, :],
                      start=True, stop=True, tile_position=(0, 0))
            _li = tt.matmul(p[64:128, :], kvc[64:128, :], q3b[64:128, :],
                      start=True, stop=True, tile_position=(64, 64))
        _i(b, _li, "pe")
    nout = b.n["pe"]

    pe0z = b.n["pe"]
    zd0 = b.n["dve"]
    zdve_p = {c: zd0 + 3 * (c + 1) for c in range(KC)}
    nz_p = {c: pe0z + (1 if c == 0 else 2 * c) for c in range(KC)}
    zb_p = {c: pe0z + (2 * c + 3 if c < KC - 1 else 16) for c in range(KC)}
    zsl = [b.zd[0:2, 128:256], b.zd[0:2, 256:384]]

    @blk.tensor
    def _(tt, s=s):
        for c in range(KC):
            bank = PS[2] if c % 2 == 0 else PS[3]
            if c >= 2:
                tt.wait_ge(b.s_dve, zdve_p[c - 2])
            tt.matmul(bank[0:2, 0:128], b.zmask[s][:, 2 * c:2 * c + 2],
                      b.q3[:, c * 128:(c + 1) * 128], start=True, stop=True)
            _li = tt.matmul(bank[0:2, 128:256], b.onesh[:],
                      b.q3[:, c * 128:(c + 1) * 128], start=True, stop=True)
            _i(b, _li, "pe")
            if c >= 1:
                tt.wait_ge(b.s_dve, zdve_p[c - 1])
                zp = PS[4] if c - 1 < 4 else PS[5]
                _i(b, tt.matmul(zp[:, ((c - 1) % 4) * 128:((c - 1) % 4 + 1) * 128],
                                b.mask2[:], zsl[(c - 1) % 2],
                                start=True, stop=True), "pe")
        tt.wait_ge(b.s_dve, zdve_p[KC - 1])
        _i(b, tt.matmul(PS[5][:, 384:512], b.mask2[:], zsl[(KC - 1) % 2],
                        start=True, stop=True), "pe")
    assert b.n["pe"] == pe0z + 16
    b.bc_zdve = dict(zdve_p)

    @blk.vector
    def _(v):
        v = _Ch(b, v, "dve")
        for c in range(KC):
            bank = PS[2] if c % 2 == 0 else PS[3]
            v.wait_ge(b.s_pe, nz_p[c])
            if c >= 2:
                v.wait_ge(b.s_pe, zb_p[c - 2])
            v.tensor_copy(b.zd[0:2, 0:128], bank[0:2, 0:128])
            v.scalar_tensor_tensor(zsl[c % 2], bank[0:2, 128:256], EPS,
                                   b.zd[0:2, 0:128], ALU.mult, ALU.add)
            v.reciprocal(zsl[c % 2], zsl[c % 2])
    assert b.n["dve"] == zd0 + 24
    nzb = pe0z + 16
    b.bc_zmm = nzb

    ow = _ow3(b, t)
    owm1 = _ow3(b, t - 1)

    @blk.scalar
    def _(sc, nout=nout, t=t):
        sc = _Ch(b, sc, "act")
        sc.wait_ge(b.s_pe, nout)
        if t >= 1:
            sc.wait_ge(b.s_dve, b.bc_s4.get(t - 2, 0))
        sc.copy(ow[:, 0:4, 1:129], PS[0][:].rearrange("p (c w) -> p c w", w=128))
        _i(b, sc.copy(ow[:, 4:8, 1:129],
                      PS[1][:].rearrange("p (c w) -> p c w", w=128)), "act")
    nev = b.n["act"]

    @blk.vector
    def _(v, nev=nev, nzb=nzb, t=t):
        v = _Ch(b, v, "dve")
        v.wait_ge(b.s_act, nev)
        v.wait_ge(b.s_pe, nzb)
        v.tensor_tensor(ow[:, 0:4, 1:129], ow[:, 0:4, 1:129],
                        PS[4][:].rearrange("p (c w) -> p c w", w=128), ALU.mult)
        v.tensor_tensor(ow[:, 4:8, 1:129], ow[:, 4:8, 1:129],
                        PS[5][:].rearrange("p (c w) -> p c w", w=128), ALU.mult)
        if t == 0:
            v.memset(ow[:, :, 0:1], 0.0)
        else:
            v.tensor_copy(ow[:, :, 0:1], owm1[:, :, 128:129])
        if t >= 1:
            v.tensor_copy(owm1[:, :, 129:130], ow[:, :, 1:2])
    b.bc_ow_ok[t] = b.n["dve"]
    if t == NT - 1:
        @blk.vector
        def _(v):
            v = _Ch(b, v, "dve")
            _i(b, v.memset(ow[:, :, 129:130], 0.0), "dve")
        b.bc_ow_ok[t] = b.n["dve"]


def _bc_S4(nc, blk, b, s, t, tap):
    PS = b.PS
    ow = _ow3(b, t)
    need = b.bc_ow_ok.get(t + 1, b.bc_ow_ok[t])

    if tap:
        _tap(nc, blk, b, "outw", b.outw[t % 2], [(b.s_dve, need)])

    act0 = b.n["act"]
    pe0 = b.n["pe"]
    dve0 = b.n["dve"]
    prj_p = {c: pe0 + c + 1 for c in range(KC)}
    nd_p = {c: dve0 + 3 * (c + 1) for c in range(KC)}

    if not hasattr(b, "bc_prj"):
        b.bc_prj = {}

    @blk.tensor
    def _(tt, t=t):
        for c in range(KC):
            tt.wait_ge(b.s_dve, nd_p[c])
            if c == 0:
                tt.wait_ge(b.s_dve, b.bc_x1_used.get(t - 2, 0))
            mA = b.mixa if c % 2 == 0 else b.mixb
            tt.matmul(PS[6][:], mA, b.wp_t[c][:, 0:512],
                      start=(c == 0), stop=(c == KC - 1))
            _li = tt.matmul(PS[7][:], mA, b.wp_t[c][:, 512:1024],
                      start=(c == 0), stop=(c == KC - 1))
            _i(b, _li, "pe")
    assert b.n["pe"] == pe0 + KC
    for c in range(KC):
        b.bc_prj[c] = prj_p[c]
    nprj = b.n["pe"]

    x1 = b.x1r[t % 2]
    @blk.vector
    def _(v, nprj=nprj, need=need, t=t):
        v = _Ch(b, v, "dve")
        v.wait_ge(b.s_dve, need)
        v.wait_ge(b.s_act, b.bc_silu.get(JC - 1, 0) if hasattr(b, "bc_silu")
                  else 0)
        for c in range(KC):
            mA = b.mixa if c % 2 == 0 else b.mixb
            mB = b.mixc if c % 2 == 0 else b.mixd
            if c >= 2:
                v.wait_ge(b.s_pe, prj_p[c - 2])
            v.scalar_tensor_tensor(mB, ow[:, c, 0:128], b.dwc_w0[:, c:c + 1],
                                   b.bdwc[:, c:c + 1].broadcast_to([128, 128]),
                                   ALU.mult, ALU.add)
            v.scalar_tensor_tensor(mA, ow[:, c, 1:129], b.dwc_w1p[:, c:c + 1],
                                   mB, ALU.mult, ALU.add)
            v.scalar_tensor_tensor(mA, ow[:, c, 2:130],
                                   b.dwc_w2[:, c:c + 1], mA,
                                   ALU.mult, ALU.add)
        v.wait_ge(b.s_pe, nprj)
        v.scalar_tensor_tensor(x1[:, 0:512], PS[6][:], 1.0,
                               b.bproj_bc[:, 0:512], ALU.mult, ALU.add)
        v.scalar_tensor_tensor(x1[:, 512:1024], PS[7][:], 1.0,
                               b.bproj_bc[:, 512:1024], ALU.mult, ALU.add)
        _i(b, v.tensor_tensor(x1, x1, b.xr[t % 2], ALU.add), "dve")
    assert b.n["dve"] == dve0 + 3 * KC + 3
    b.bc_x1 = b.n["dve"]
    b.bc_s4[t] = b.n["dve"]
    b.bc_x_used.setdefault(t, []).append((b.s_dve, b.n["dve"]))
    if tap:
        _tap(nc, blk, b, "x1", x1, [(b.s_dve, b.bc_x1)])


def _bc_S5(nc, blk, b, s, t, tap):
    PS = b.PS
    x1 = b.x1r[t % 2]
    nx1 = b.bc_x1
    tm = b.tmp

    act0 = b.n["act"]

    @blk.vector
    def _(v, nx1=nx1, act0=act0):
        v = _Ch(b, v, "dve")
        v.wait_ge(b.s_dve, nx1)
        v.reduce_sum(tm[:, 4:5], x1, axis=AX.X)
        v.wait_ge(b.s_act, act0 + 1)
        v.tensor_scalar_mul(tm[:, 4:5], tm[:, 4:5], 1.0 / C)
        v.tensor_scalar_mul(tm[:, 9:10], tm[:, 8:9], 1.0 / C)
        v.tensor_tensor(tm[:, 10:11], tm[:, 4:5], tm[:, 4:5], ALU.mult)
        v.scalar_tensor_tensor(tm[:, 11:12], tm[:, 10:11], -1.0, tm[:, 9:10],
                               ALU.mult, ALU.add)
        v.tensor_scalar_add(tm[:, 11:12], tm[:, 11:12], EPS_LN)
        _newton_rsqrt(v, tm[:, 6:7], tm[:, 11:12], tm[:, 12:13], tm[:, 13:14])
        v.tensor_scalar_mul(tm[:, 5:6], tm[:, 4:5], -1.0)
        b.bc_nm2 = b.n["dve"]
        v.wait_ge(b.s_act, act0 + 2)
        v.reduce_max(tm[:, 7:8], b.scratch, axis=AX.X, apply_absolute_value=True)
        v.tensor_tensor(tm[:, 15:16], tm[:, 6:7], tm[:, 7:8], ALU.mult)
        v.tensor_scalar_max(tm[:, 16:17], tm[:, 15:16], 1e-5)
        v.reciprocal(tm[:, 14:15], tm[:, 16:17])
        v.tensor_scalar(tm[:, 14:15], tm[:, 14:15], tm[:, 6:7], 127.0,
                        ALU.mult, ALU.mult)
        _li = v.tensor_scalar(tm[:, 17:18], tm[:, 16:17], b.wsc[:, 2:3], 1.0 / 127.0,
                        ALU.mult, ALU.mult)
        _i(b, _li, "dve")
    nm2 = b.bc_nm2
    nct = b.n["dve"]

    @blk.scalar
    def _(sc, nx1=nx1, t=t):
        sc = _Ch(b, sc, "act")
        sc.wait_ge(b.s_dve, nx1)
        sc.wait_ge(b.s_dve, getattr(b, 'bc_mix', b.bc_q3))
        sc.wait_ge(b.s_pe, b.bc_zmm)
        sc.wait_ge(b.s_st, b.bc_scr_free)
        sc.activation(b.scratch, x1, AF.Square, accum_out=tm[:, 8:9])
        sc.wait_ge(b.s_dve, b.bc_nm2)
        _i(b, sc.activation(b.scratch, x1, AF.Identity, bias=tm[:, 5:6],
                            scale=1.0), "act")
    assert b.n["act"] == act0 + 2

    @blk.scalar
    def _(sc, nct=nct):
        sc = _Ch(b, sc, "act")
        for hh in range(2):
            sh = b.scratch[:, hh * 512:(hh + 1) * 512]
            if hh == 0:
                sc.wait_ge(b.s_dve, nct)
                sc.wait_ge(b.s_dve, getattr(b, "bc_qdve", 0))
            else:
                sc.wait_ge(b.s_dve, nct + 4)
            sc.activation(sh, sh, AF.Identity, scale=tm[:, 14:15])
            _i(b, sc.copy(b.i32b[:, :], sh), "act")
    assert b.n["act"] == act0 + 6

    ntp = None
    for hh in range(2):
        @blk.vector
        def _(v, hh=hh, act0=act0):
            v = _Ch(b, v, "dve")
            v.wait_ge(b.s_act, act0 + 4 + 2 * hh)
            if hh == 1:
                v.wait_ge(b.s_pe, b.bc_s5h)
            sh = b.scratch[:, hh * 512:(hh + 1) * 512]
            r6 = PS[6][:, 0:512]
            v.tensor_tensor(r6, sh, b.i32b[:, :], ALU.subtract)
            v.tensor_single_scalar(b.xqtm[:, :], r6, 0.5, ALU.is_ge)
            v.scalar_tensor_tensor(b.xqtm[:, :], r6, -0.5, b.xqtm[:, :],
                                   ALU.is_le, ALU.subtract)
            _i(b, v.tensor_tensor(b.xqtm[:, :], b.i32b[:, :], b.xqtm[:, :],
                                  ALU.subtract), "dve")
        nqz = b.n["dve"]
        if hh == 0:
            assert nqz == nct + 4

        @blk.tensor
        def _(tt, hh=hh, nqz=nqz, t=t):
            tt.wait_ge(b.s_dve, nqz)
            if hh == 0 and t >= 1:
                tt.wait_ge(b.s_act, b.bc_xw_ok.get(t - 1, 0))
            p2b = PS[2][:, :].bitcast(BF16)
            for k in range(4):
                _li = tt.transpose(p2b[:, (hh * 4 + k) * 128:(hh * 4 + k + 1) * 128],
                                   b.xqtm[:, k * 128:(k + 1) * 128], b.idb[:])
            if hh == 1:
                _li = tt.transpose(PS[3][0:1, 0:128], tm[:, 17:18], b.idf[:])
            _i(b, _li, "pe")
        b.bc_s5h = b.n["pe"]
    ntp = b.n["pe"]
    b.bc_s5qz = b.n["dve"]

    xw = _xw3(b, t)
    xwm1 = _xw3(b, t - 1)
    cw = _cwin(b, t)
    cwm1 = _cwin(b, t - 1)

    @blk.scalar
    def _(sc, ntp=ntp, t=t):
        sc = _Ch(b, sc, "act")
        sc.wait_ge(b.s_pe, ntp)
        if t >= 1:
            sc.wait_ge(b.s_dve, b.bc_s6.get(t - 2, 0))
        p2b = PS[2][:, :].bitcast(BF16).rearrange("p (c w) -> p c w", w=128)
        sc.copy(xw[:, :, 1:129], p2b)
        _i(b, sc.copy(cw[0:1, 1:129], PS[3][0:1, 0:128]), "act")
    assert b.n["act"] == act0 + 8
    ncp = b.n["act"]

    @blk.vector
    def _(v, ncp=ncp, t=t):
        v = _Ch(b, v, "dve")
        v.wait_ge(b.s_act, ncp)
        if t == 0:
            v.memset(xw[:, :, 0:1], 0.0)
            v.memset(cw[0:1, 0:1], 1.0)
        else:
            v.tensor_copy(xw[:, :, 0:1], xwm1[:, :, 128:129])
            v.tensor_copy(cw[0:1, 0:1], cwm1[0:1, 128:129])
            v.tensor_copy(xwm1[:, :, 129:130], xw[:, :, 1:2])
            v.tensor_copy(cwm1[0:1, 129:130], cw[0:1, 1:2])
        if t == NT - 1:
            v.memset(xw[:, :, 129:130], 0.0)
            v.memset(cw[0:1, 129:130], 1.0)
    b.bc_xw_ok[t] = b.n["act"]
    b.bc_xw_dve = b.n["dve"]
    if tap:
        _tap(nc, blk, b, "xq", b.xqw[t % 2], [(b.s_dve, b.bc_xw_dve)])


def _bc_S6(nc, blk, b, s, t, tap):
    PS = b.PS
    xw = _xw3(b, t)
    cw = _cwin(b, t)
    tm = b.tmp
    ndep = b.bc_xw_dve

    @blk.tensor
    def _(tt, ndep=ndep):
        tt.wait_ge(b.s_dve, ndep)
        _i(b, tt.matmul(PS[3][:, 128:258], b.ones_row[:], cw[0:1, 0:130],
                        start=True, stop=True), "pe")
    ncb = b.n["pe"]

    @blk.scalar
    def _(sc, ncb=ncb):
        sc = _Ch(b, sc, "act")
        sc.wait_ge(b.s_pe, ncb)
        _i(b, sc.copy(b.cbc, PS[3][:, 128:258]), "act")
    ncbc = b.n["act"]
    if tap:
        _tap(nc, blk, b, "cbc", b.cbc, [(b.s_act, ncbc)])

    act0 = b.n["act"]
    dve0 = b.n["dve"]
    silu_p = {j: act0 + j + 1 for j in range(JC)}
    hevac_p = {j: dve0 + 1 + 4 * j for j in range(JC)}
    cvC_p = {j: dve0 + 4 * (j + 1) for j in range(JC)}

    nh = {}
    @blk.tensor
    def _(tt, ncbc=ncbc):
        for jc in range(JC):
            hp = PS[2 + jc % 4]
            if jc < 4:
                tt.wait_ge(b.s_act, ncbc)
            else:
                tt.wait_ge(b.s_dve, hevac_p[jc - 4])
            for k in range(KC):
                _li = tt.matmul(hp[:, 0:130], b.w1q[k][:, jc * 128:(jc + 1) * 128],
                                xw[:, k, 0:130], start=(k == 0), stop=(k == KC - 1))
            _i(b, _li, "pe")
            nh[jc] = b.n["pe"]

    def _conv(v, j):
        mA = b.mixa if j % 2 == 0 else b.mixb
        mB = b.mixc if j % 2 == 0 else b.mixd
        if j >= 2:
            v.wait_ge(b.s_act, silu_p[j - 2])
        v.scalar_tensor_tensor(mB, b.s_loc[j][:, 0:128], b.cw0[:, j:j + 1],
                               b.cbf[:, j:j + 1].broadcast_to([128, 128]),
                               ALU.mult, ALU.add)
        v.scalar_tensor_tensor(mA, b.s_loc[j][:, 1:129], b.cw1[:, j:j + 1],
                               mB, ALU.mult, ALU.add)
        v.scalar_tensor_tensor(mA, b.s_loc[j][:, 2:130], b.cw2[:, j:j + 1],
                               mA, ALU.mult, ALU.add)

    sgroups = [(0, 7), (7, 7), (14, 7), (21, 7), (28, 2), (30, 2)]
    @blk.vector
    def _(v):
        v = _Ch(b, v, "dve")
        for jc in range(JC):
            if jc >= 1:
                _conv(v, jc - 1)
            v.wait_ge(b.s_pe, nh[jc])
            v.tensor_tensor(b.s_loc[jc], PS[2 + jc % 4][:, 0:130], b.cbc,
                            ALU.mult)
        _conv(v, JC - 1)
        first = True
        for g0, glen in sgroups:
            base = b.s_loc[g0]
            src = bass.AP(base.tensor, base.offset,
                          [list(base.ap[0]), [1, 128], [130, glen]])
            dst = b.M1 if first else b.mixc
            v.wait_ge(b.s_act, silu_p[g0 + glen - 1])
            v.reduce_max(dst, src, axis=AX.X, apply_absolute_value=True)
            if not first:
                v.tensor_max(b.M1, b.M1, b.mixc)
            first = False
    assert b.n["dve"] == dve0 + 139, b.n["dve"] - dve0
    b.bc_hevac = dict(hevac_p)
    nmx = b.n["dve"]

    @blk.scalar
    def _(sc):
        sc = _Ch(b, sc, "act")
        for jc in range(JC):
            sc.wait_ge(b.s_dve, cvC_p[jc])
            sc.activation(b.s_loc[jc][:, 0:128],
                          b.mixa if jc % 2 == 0 else b.mixb, AF.Silu)
    assert b.n["act"] == act0 + 32, b.n["act"] - act0
    b.bc_silu = dict(silu_p)

    @blk.tensor
    def _(tt, nmx=nmx):
        tt.wait_ge(b.s_dve, nmx)
        _i(b, tt.transpose(PS[3][:, 258:386], b.M1, b.idf[:]), "pe")
    nmt = b.n["pe"]

    @blk.vector
    def _(v, nmt=nmt):
        v = _Ch(b, v, "dve")
        v.wait_ge(b.s_pe, nmt)
        v.reduce_max(tm[:, 20:21], PS[3][:, 258:386], axis=AX.X)
        v.tensor_scalar_max(tm[:, 21:22], tm[:, 20:21], 1e-5)
        v.reciprocal(tm[:, 22:23], tm[:, 21:22])
        v.tensor_scalar_mul(tm[:, 22:23], tm[:, 22:23], 127.0)
        _i(b, v.tensor_scalar(tm[:, 23:24], tm[:, 21:22], b.wsc[:, 3:4],
                              1.0 / 127.0, ALU.mult, ALU.mult), "dve")
    ns2 = b.n["dve"]
    b.bc_c2 = ns2

    @blk.tensor
    def _(tt, ns2=ns2):
        tt.wait_ge(b.s_dve, ns2)
        _i(b, tt.transpose(PS[3][0:1, 0:128], tm[:, 22:23], b.idf[:]), "pe")
    ns2t = b.n["pe"]

    @blk.vector
    def _(v, ns2t=ns2t):
        v = _Ch(b, v, "dve")
        v.wait_ge(b.s_pe, ns2t)
        _i(b, v.tensor_copy(b.zd[0:1, 0:128], PS[3][0:1, 0:128]), "dve")
    ns2r = b.n["dve"]

    @blk.tensor
    def _(tt, ns2r=ns2r):
        tt.wait_ge(b.s_dve, ns2r)
        _i(b, tt.matmul(PS[2][:, 0:128], b.ones_row[:], b.zd[0:1, 0:128],
                        start=True, stop=True), "pe")
    ns2b = b.n["pe"]

    @blk.vector
    def _(v, ns2b=ns2b):
        v = _Ch(b, v, "dve")
        v.wait_ge(b.s_pe, ns2b)
        _i(b, v.tensor_copy(b.s2b, PS[2][:, 0:128]), "dve")
    ns2s = b.n["dve"]

    groups = [(0, 4), (4, 3), (7, 4), (11, 3), (14, 4), (18, 3),
              (21, 4), (25, 3), (28, 2), (30, 2)]
    NG = len(groups)
    pq0 = b.n["pool"]
    dq0 = b.n["dve"]
    pw0 = b.n["pe"]
    dqe_p = {gi: dq0 + 4 * (gi + 1) for gi in range(NG)}
    w2_p = {gi: pw0 + gi + 1 for gi in range(NG)}
    srcs = []
    for g0, glen in groups:
        base = b.s_loc[g0]
        srcs.append(bass.AP(base.tensor, base.offset,
                            [list(base.ap[0]), [130, glen], [1, 128]]))
    if not hasattr(b, "bc_w2"):
        b.bc_w2 = {}

    @blk.gpsimd
    def _(g, ns2s=ns2s):
        g = _Ch(b, g, "pool")
        g.wait_ge(b.s_dve, ns2s)
        for gi, (g0, glen) in enumerate(groups):
            g.tensor_tensor(srcs[gi], srcs[gi],
                            b.s2b.unsqueeze(1).broadcast_to([128, glen, 128]),
                            ALU.mult)
    assert b.n["pool"] == pq0 + NG

    aq0 = b.n["act"]
    @blk.scalar
    def _(sc):
        sc = _Ch(b, sc, "act")
        for gi, (g0, glen) in enumerate(groups):
            sc.wait_ge(b.s_pool, pq0 + gi + 1)
            if gi >= 1:
                sc.wait_ge(b.s_dve, dqe_p[gi - 1])
            i3 = b.i32b[:, 0:glen * 128].rearrange("p (c w) -> p c w", w=128)
            sc.copy(i3, srcs[gi])
    assert b.n["act"] == aq0 + NG

    @blk.vector
    def _(v, t=t):
        v = _Ch(b, v, "dve")
        for gi, (g0, glen) in enumerate(groups):
            sqb = b.sq[gi % 2]
            src = srcs[gi]
            v.wait_ge(b.s_act, aq0 + gi + 1)
            if gi >= 2:
                v.wait_ge(b.s_pe, w2_p[gi - 2])
            else:
                v.wait_ge(b.s_pe, b.bc_w2.get(8 if gi == 0 else 9, 0))
            i32v = b.i32b[:, 0:glen * 128]
            r3 = PS[4][:, 0:glen * 128].rearrange("p (c w) -> p c w", w=128)
            g3 = b.xqtm[:, 0:glen * 128].rearrange("p (c w) -> p c w", w=128)
            i3 = i32v.rearrange("p (c w) -> p c w", w=128)
            v.tensor_tensor(r3, src, i3, ALU.subtract)
            v.tensor_single_scalar(g3, r3, 0.5, ALU.is_ge)
            v.scalar_tensor_tensor(g3, r3, -0.5, g3, ALU.is_le, ALU.subtract)
            _i(b, v.tensor_tensor(sqb[:, 0:glen * 128], i32v,
                                  b.xqtm[:, 0:glen * 128], ALU.subtract), "dve")
    assert b.n["dve"] == dq0 + 4 * NG
    b.bc_qdve = b.n["dve"]

    b.bc_w2 = {}
    @blk.tensor
    def _(tt, t=t):
        for gi, (g0, glen) in enumerate(groups):
            sqb = b.sq[gi % 2]
            tt.wait_ge(b.s_dve, dqe_p[gi])
            if g0 == 0:
                tt.wait_ge(b.s_dve, b.bc_s4.get(t + 1, b.bc_s4.get(t, 0)))
                tt.wait_ge(b.s_dve, b.bc_x1_used.get(t - 1, 0))
            for j in range(g0, g0 + glen):
                lh = sqb[:, (j - g0) * 128:(j - g0 + 1) * 128]
                tt.matmul(PS[6][:], lh, b.w2q[j][:, 0:512],
                          start=(j == 0), stop=(j == JC - 1))
                _li = tt.matmul(PS[7][:], lh, b.w2q[j][:, 512:1024],
                          start=(j == 0), stop=(j == JC - 1))
            _i(b, _li, "pe")
            b.bc_w2[gi] = b.n["pe"]
    assert b.n["pe"] == pw0 + NG
    b.bc_out2 = b.n["pe"]
    b.bc_s6[t] = b.n["pe"]


def _bc_S7(nc, blk, b, s, t):
    PS = b.PS
    tm = b.tmp
    x1 = b.x1r[t % 2]
    nw2 = b.bc_out2

    @blk.scalar
    def _(sc, nw2=nw2, t=t):
        sc = _Ch(b, sc, "act")
        sc.wait_ge(b.s_pe, nw2)
        sc.wait_ge(b.s_dve, max(b.bc_c2, b.bc_s5qz))
        sc.wait_ge(b.s_st, b.bc_scr_free)
        sc.activation(b.x2buf[:, 0:512], PS[6][:], AF.Identity,
                      scale=tm[:, 23:24])
        _i(b, sc.activation(b.x2buf[:, 512:1024], PS[7][:], AF.Identity,
                            scale=tm[:, 23:24]), "act")
    nsc2 = b.n["act"]

    @blk.vector
    def _(v, nsc2=nsc2, t=t):
        v = _Ch(b, v, "dve")
        v.wait_ge(b.s_act, nsc2)
        _i(b, v.tensor_tensor(b.x2buf, b.x2buf, x1, ALU.add), "dve")
    nx2 = b.n["dve"]
    b.bc_x1_used[t] = nx2

    @blk.gpsimd
    def _(g, nx2=nx2, t=t, s=s):
        g.wait_ge(b.s_dve, nx2)
        g.wait_ge(b.s_st, b.n["st"])
        _i(b, g.dma_start(out=b.y_d[s * SEQ + t * T:s * SEQ + (t + 1) * T, :],
                          in_=b.x2buf), "st", 16)
    b.bc_scr_free = b.n["st"]


def _emit_finish(nc, b):
    with nc.Block() as blk:
        @blk.gpsimd
        def _(g):
            g.wait_ge(b.s_st, b.n["st"])
            g.wait_ge(b.s_ld, b.n["ld"])


def kernel(L0_attn_x, L0_attn_w_qkv, L0_attn_w_proj, L0_attn_b_proj,
           L0_attn_w_dwc, L0_attn_b_dwc, L0_ffn_w1, L0_ffn_conv_w,
           L0_ffn_conv_b, L0_ffn_w2):
    import ml_dtypes
    x = np.ascontiguousarray(np.asarray(L0_attn_x, dtype=np.float32))
    shards = [np.ascontiguousarray(x[2 * i:2 * i + 2].reshape(NSEQ * SEQ, C))
              for i in range(N_CORES)]
    wq = np.ascontiguousarray(np.asarray(L0_attn_w_qkv, dtype=np.float32))
    wp = np.ascontiguousarray(np.asarray(L0_attn_w_proj, dtype=np.float32))
    bp = np.ascontiguousarray(np.asarray(L0_attn_b_proj, dtype=np.float32).reshape(1, C))
    wd = np.ascontiguousarray(np.asarray(L0_attn_w_dwc, dtype=np.float32))
    bd = np.ascontiguousarray(np.asarray(L0_attn_b_dwc, dtype=np.float32).reshape(C, 1))
    w1 = np.ascontiguousarray(np.asarray(L0_ffn_w1, dtype=np.float32))
    cw = np.ascontiguousarray(np.asarray(L0_ffn_conv_w, dtype=np.float32).reshape(HID, 3))
    cb = np.ascontiguousarray(np.asarray(L0_ffn_conv_b, dtype=np.float32).reshape(HID, 1))
    w2 = np.ascontiguousarray(np.asarray(L0_ffn_w2, dtype=np.float32))
    idf = np.eye(128, dtype=np.float32)
    idb = np.eye(128, dtype=ml_dtypes.bfloat16)
    cmask = np.zeros((2, 128), dtype=np.float32)
    cmask[0, 0:64] = 1.0
    cmask[1, 64:128] = 1.0

    nc = build(taps=_TAPS, maxphase=_MAXPHASE)
    in_maps = [{"x": shards[i], "w_qkv": wq, "w_proj": wp, "b_proj": bp,
                "w_dwc": wd, "b_dwc": bd, "w1": w1, "conv_w": cw, "conv_b": cb,
                "w2": w2, "idf": idf, "idb": idb, "cmask": cmask} for i in range(N_CORES)]
    res = run_bass_kernel_spmd(nc, in_maps, core_ids=list(range(N_CORES)))
    out = np.concatenate([np.asarray(r["y"]).reshape(NSEQ, SEQ, C)
                          for r in res.results], axis=0)
    return out.astype(np.float32)


# version 21 (final)
# speedup vs baseline: 1.0281x; 1.0281x over previous
import numpy as np
import contextlib
import concourse.bass as bass
import concourse.mybir as mybir
from concourse.bass_utils import run_bass_kernel_spmd

F32 = mybir.dt.float32
BF16 = mybir.dt.bfloat16
FP8 = mybir.dt.float8e4
I32 = mybir.dt.int32
AF = mybir.ActivationFunctionType
ALU = mybir.AluOpType
AX = mybir.AxisListType

SEQ, C, H, D, HID, T = 2048, 1024, 16, 64, 4096, 128
NT = SEQ // T
NSEQ = 2
KC = C // 128
JC = HID // 128
EPS_LN = 1e-5
EPS = 1e-6
NEWT = 7
N_CORES = 8

_TAPS = ()
_MAXPHASE = 99


class _B:
    pass


def _mk(nc, b, es):
    sb = lambda name, shape, dt: es.enter_context(nc.sbuf_tensor(name, shape, dt))
    b.wq_t = [sb(f"wq_t{k}", [128, C], F32) for k in range(KC)]
    b.wp_t = [sb(f"wp_t{k}", [128, C], F32) for k in range(KC)]
    b.w1q = [sb(f"w1q{k}", [128, HID], FP8) for k in range(KC)]
    b.w2q = [sb(f"w2q{j}", [128, C], FP8) for j in range(JC)]
    b.arena = [sb(f"arena{k}", [128, 2048], F32) for k in range(KC)]
    b.idf = sb("idf_s", [128, 128], F32)
    b.idb = sb("idb_s", [128, 128], BF16)
    b.ones_col = sb("ones_col", [128, 1], F32)
    b.ones_row = sb("ones_row", [1, 128], F32)
    b.mask2 = sb("mask2", [2, 128], F32)
    b.onesh = sb("onesh", [128, 2], F32)
    b.dwc_w0 = sb("dwc_w0", [128, 8], F32)
    b.dwc_w1p = sb("dwc_w1p", [128, 8], F32)
    b.dwc_w2 = sb("dwc_w2", [128, 8], F32)
    b.bdwc = sb("bdwc", [128, 8], F32)
    b.cw0 = sb("cw0", [128, 32], F32)
    b.cw1 = sb("cw1", [128, 32], F32)
    b.cw2 = sb("cw2", [128, 32], F32)
    b.cbf = sb("cbf", [128, 32], F32)
    b.bproj_bc = sb("bproj_bc", [128, C], F32)
    b.kv_sb = [sb(f"kv_sb{s}", [128, 512], F32) for s in range(NSEQ)]
    b.zmask = [sb(f"zmask{s}", [128, 16], F32) for s in range(NSEQ)]
    b.wsc = sb("wsc", [128, 4], F32)
    b.asum = sb("asum", [128, 64], F32)
    b.sml = sb("sml", [1, 8], F32)
    b.tmp = sb("tmp", [128, 32], F32)
    b.zd = sb("zd", [2, 384], F32)
    b.mixa = sb("mixa", [128, 128], F32)[:, :]
    b.mixb = sb("mixb", [128, 128], F32)[:, :]
    b.mixc = sb("mixc", [128, 128], F32)[:, :]
    b.mixd = sb("mixd", [128, 128], F32)[:, :]

    A = b.arena
    b.bproj_r = A[7][0:1, 0:1024]
    b.dwc = A[6][:, 0:24]
    b.cwf = A[6][:, 24:120]
    b.outw = [A[0][:, 0:1040], A[1][:, 0:1040]]
    b.s_loc = []
    for j in range(JC):
        if j < 28:
            sl, off = j // 7, (1040 if j < 14 else 1024) + (j % 7) * 130
        else:
            sl, off = 4 + (j - 28) // 2, 1024 + ((j - 28) % 2) * 130
        b.s_loc.append(A[sl][:, off:off + 130])
    b.x1r = [A[2][:, 0:1024], A[3][:, 0:1024]]
    b.xr = [A[4][:, 0:1024], A[5][:, 0:1024]]
    b.xqw = [A[4][:, 1284:1812].bitcast(BF16),
             A[5][:, 1284:1812].bitcast(BF16)]
    b.xnfm = A[6][:, 0:1024]
    b.q3 = A[7][:, 1024:2048]
    b.i32b = A[6][:, 1024:1536].bitcast(I32)
    b.xqtm = A[6][:, 1536:1792].bitcast(BF16)
    b.M1 = A[6][:, 1792:1920]
    b.s2b = A[6][:, 1920:2048]
    b.scratch = A[7][:, 1024:2048]
    b.xntm = b.scratch
    b.sq = [A[7][:, 0:256].bitcast(BF16), A[7][:, 256:512].bitcast(BF16)]
    b.cbc = A[7][:, 512:642]
    b.cw_win = A[7][0:1, 642:906]
    b.x2buf = b.scratch

    b.PS = [es.enter_context(nc.psum_tensor(f"ps{i}", [128, 512], F32))
            for i in range(8)]

    b.s_ld = es.enter_context(nc.semaphore("s_ld"))
    b.s_ldb = es.enter_context(nc.semaphore("s_ldb"))
    b.s_st = es.enter_context(nc.semaphore("s_st"))
    b.s_pe = es.enter_context(nc.semaphore("s_pe"))
    b.s_dve = es.enter_context(nc.semaphore("s_dve"))
    b.s_act = es.enter_context(nc.semaphore("s_act"))
    b.s_pool = es.enter_context(nc.semaphore("s_pool"))
    b.n = {"ld": 0, "ldb": 0, "st": 0, "pe": 0, "dve": 0, "act": 0, "pool": 0}
    b.sem = {"ld": b.s_ld, "ldb": b.s_ldb, "st": b.s_st, "pe": b.s_pe,
             "dve": b.s_dve, "act": b.s_act, "pool": b.s_pool}


class _Ch:
    _PASS = ("wait_ge", "nop", "sem_inc", "sem_clear", "drain")

    def __init__(self, b, eng, key):
        self._b, self._e, self._k = b, eng, key

    def __getattr__(self, name):
        f = getattr(self._e, name)
        if name in _Ch._PASS or not callable(f):
            return f
        b, e, k = self._b, self._e, self._k

        def w(*a, **kw):
            if b.n[k] > 0:
                e.wait_ge(b.sem[k], b.n[k])
            inst = f(*a, **kw)
            inst.then_inc(b.sem[k], 1)
            b.n[k] += 1
            return inst
        return w


def _i(b, inst, eng, n=1):
    if eng in ("dve", "act", "pool"):
        return b.n[eng]
    inst.then_inc(b.sem[eng], n)
    b.n[eng] += n
    return b.n[eng]


def _pload(b, sy, out, in_, parity):
    eng = "ld" if parity == 0 else "ldb"
    sy.wait_ge(b.sem[eng], b.n[eng])
    _i(b, sy.dma_start(out=out, in_=in_), eng, 16)
    return (b.sem[eng], b.n[eng])


def _round_robust(v, b, t_ap, i32_ap, r_ap, g_ap, out_ap):
    v.tensor_copy(i32_ap, t_ap)
    v.tensor_tensor(r_ap, t_ap, i32_ap, ALU.subtract)
    v.tensor_single_scalar(g_ap, r_ap, 0.5, ALU.is_ge)
    v.scalar_tensor_tensor(g_ap, r_ap, -0.5, g_ap, ALU.is_le,
                           ALU.subtract)
    return v.tensor_tensor(out_ap, i32_ap, g_ap, ALU.subtract)


def _tap(nc, blk, b, name, src, waits=()):
    if name not in b.tapd:
        return
    @blk.gpsimd
    def _(g):
        for sem, val in waits:
            g.wait_ge(sem, val)
        g.wait_ge(b.s_st, b.n["st"])
        _i(b, g.dma_start(out=b.tapd[name], in_=src), "st", 16)
    nst = b.n["st"]

    @blk.vector
    def _(v):
        v.wait_ge(b.s_st, nst)

    @blk.scalar
    def _(sc):
        sc.wait_ge(b.s_st, nst)


def _phase_gate(nc, blk, b):
    @blk.sync
    def _(sy):
        sy.wait_ge(b.s_ld, b.n["ld"])
        sy.wait_ge(b.s_ldb, b.n["ldb"])


def _newton_rsqrt(v, out_c, w_c, t1, t2):
    v.tensor_scalar(out_c, w_c, -0.5, 1.5, ALU.mult, ALU.add)
    li = v.tensor_scalar(out_c, out_c, 0.18, 1.4, ALU.max, ALU.min)
    for _ in range(NEWT):
        v.tensor_tensor(t1, out_c, out_c, ALU.mult)
        v.tensor_tensor(t2, t1, w_c, ALU.mult)
        v.tensor_scalar(t2, t2, -0.5, 1.5, ALU.mult, ALU.add)
        li = v.tensor_tensor(out_c, out_c, t2, ALU.mult)
    return li


def build(taps=(), maxphase=99):
    nc = bass.Bass("TRN2", target_bir_lowering=False, detect_race_conditions=True)
    b = _B()
    b.taps = set(taps)

    b.x_d = nc.dram_tensor("x", [NSEQ * SEQ, C], F32, kind="ExternalInput").ap()
    b.wqkv_d = nc.dram_tensor("w_qkv", [3 * C, C], F32, kind="ExternalInput").ap()
    b.wproj_d = nc.dram_tensor("w_proj", [C, C], F32, kind="ExternalInput").ap()
    b.bproj_d = nc.dram_tensor("b_proj", [1, C], F32, kind="ExternalInput").ap()
    b.wdwc_d = nc.dram_tensor("w_dwc", [C, 3], F32, kind="ExternalInput").ap()
    b.bdwc_d = nc.dram_tensor("b_dwc", [C, 1], F32, kind="ExternalInput").ap()
    b.w1_d = nc.dram_tensor("w1", [HID, C], F32, kind="ExternalInput").ap()
    b.cw_d = nc.dram_tensor("conv_w", [HID, 3], F32, kind="ExternalInput").ap()
    b.cb_d = nc.dram_tensor("conv_b", [HID, 1], F32, kind="ExternalInput").ap()
    b.w2_d = nc.dram_tensor("w2", [C, HID], F32, kind="ExternalInput").ap()
    b.idf_d = nc.dram_tensor("idf", [128, 128], F32, kind="ExternalInput").ap()
    b.idb_d = nc.dram_tensor("idb", [128, 128], BF16, kind="ExternalInput").ap()
    b.cmask_d = nc.dram_tensor("cmask", [2, 128], F32, kind="ExternalInput").ap()
    b.y_d = nc.dram_tensor("y", [NSEQ * SEQ, C], F32, kind="ExternalOutput").ap()

    b.tapd = {}
    tapshapes = {"kv0": ([128, 512], F32), "zmask0": ([128, 16], F32),
                 "xnfm": ([128, 1024], F32), "q3": ([128, 1024], F32),
                 "outw": ([128, 1040], F32), "x1": ([128, 1024], F32),
                 "xq": ([128, 1056], BF16), "sfull": ([128, 520], F32),
                 "m1": ([128, 128], F32), "wsc": ([128, 4], F32),
                 "cbc": ([128, 130], F32),
                 "w1q": ([128, 4096], F32), "w2q": ([128, 1024], F32)}
    for name in b.taps:
        shape, dt = tapshapes[name]
        b.tapd[name] = nc.dram_tensor("tap_" + name, shape, dt,
                                      kind="ExternalOutput").ap()

    with contextlib.ExitStack() as es:
        _mk(nc, b, es)
        _emit_P0(nc, b)
        if maxphase >= 1:
            _emit_wqkvT(nc, b)
        if maxphase >= 2:
            _emit_ffn_quant(nc, b)
        if maxphase >= 3:
            _emit_A(nc, b, 0)
        if maxphase >= 4:
            _emit_A(nc, b, 1)
        if maxphase >= 5:
            _emit_wprojT(nc, b)
        if maxphase >= 6:
            _emit_BC(nc, b, 0)
        if maxphase >= 7:
            _emit_BC(nc, b, 1)
        _emit_finish(nc, b)
    return nc


def _emit_P0(nc, b):
    with nc.Block() as blk:
        @blk.sync
        def _(sy):
            dma = lambda o, i_: _i(b, sy.dma_start(out=o, in_=i_), "ld", 16)
            dma(b.idf[:], b.idf_d)
            dma(b.idb[:], b.idb_d)
            dma(b.mask2[:], b.cmask_d)
            for c in range(8):
                dma(b.dwc[:, c * 3:(c + 1) * 3], b.wdwc_d[c * 128:(c + 1) * 128, :])
                dma(b.bdwc[:, c:c + 1], b.bdwc_d[c * 128:(c + 1) * 128, :])
            for j in range(JC):
                dma(b.cwf[:, j * 3:(j + 1) * 3], b.cw_d[j * 128:(j + 1) * 128, :])
                dma(b.cbf[:, j:j + 1], b.cb_d[j * 128:(j + 1) * 128, :])
            dma(b.bproj_r[:], b.bproj_d)

        @blk.vector
        def _(v):
            v = _Ch(b, v, "dve")
            v.memset(b.ones_col[:], 1.0)
            v.memset(b.ones_row[:], 1.0)
            v.memset(b.onesh[:], 0.0)
            v.memset(b.onesh[0:64, 0:1], 1.0)
            v.memset(b.onesh[64:128, 1:2], 1.0)
            v.wait_ge(b.s_ld, b.n["ld"])
            dv = b.dwc.rearrange("p (c j) -> p c j", j=3)
            v.tensor_copy(b.dwc_w0[:], dv[:, :, 0])
            v.tensor_scalar_add(b.dwc_w1p[:], dv[:, :, 1], 1.0)
            v.tensor_copy(b.dwc_w2[:], dv[:, :, 2])
            cv = b.cwf.rearrange("p (c j) -> p c j", j=3)
            v.tensor_copy(b.cw0[:], cv[:, :, 0])
            v.tensor_copy(b.cw1[:], cv[:, :, 1])
            _i(b, v.tensor_copy(b.cw2[:], cv[:, :, 2]), "dve")
        ndve0 = b.n["dve"]

        @blk.tensor
        def _(t, ndve0=ndve0):
            t.wait_ge(b.s_ld, b.n["ld"])
            t.wait_ge(b.s_dve, ndve0)
            t.matmul(b.PS[3][:], b.ones_row[:], b.bproj_r[0:1, 0:512],
                     start=True, stop=True)
            _i(b, t.matmul(b.PS[4][:], b.ones_row[:], b.bproj_r[0:1, 512:1024],
                           start=True, stop=True), "pe")

        @blk.scalar
        def _(s):
            s = _Ch(b, s, "act")
            s.wait_ge(b.s_pe, b.n["pe"])
            s.copy(b.bproj_bc[:, 0:512], b.PS[3][:])
            _i(b, s.copy(b.bproj_bc[:, 512:1024], b.PS[4][:]), "act")


def _emit_wqkvT(nc, b):
    PS = b.PS
    with nc.Block() as blk:
        _phase_gate(nc, blk, b)
        evq = {}
        for i in range(24):
            @blk.sync
            def _(sy, i=i):
                if i >= 2:
                    sy.wait_ge(b.s_act, evq[i - 2])
                b.lw = _pload(b, sy, b.wp_t[i % 2][:],
                              b.wqkv_d[i * 128:(i + 1) * 128, :], i % 2)
            lw = b.lw

            @blk.tensor
            def _(t, i=i, lw=lw):
                t.wait_ge(*lw)
                if i >= 1:
                    t.wait_ge(b.s_act, evq[i - 1])
                for k in range(KC):
                    p = PS[k // 4][:, (k % 4) * 128:(k % 4 + 1) * 128]
                    _li = t.transpose(p, b.wp_t[i % 2][:, k * 128:(k + 1) * 128], b.idf[:])
                _i(b, _li, "pe")
            npe = b.n["pe"]

            @blk.scalar
            def _(s, i=i, npe=npe):
                s = _Ch(b, s, "act")
                s.wait_ge(b.s_pe, npe)
                for k in range(KC):
                    p = PS[k // 4][:, (k % 4) * 128:(k % 4 + 1) * 128]
                    dest = (b.wq_t[k][:, i * 128:(i + 1) * 128] if i < 8 else
                            b.arena[k][:, (i - 8) * 128:(i - 7) * 128])
                    _li = s.copy(dest, p)
                _i(b, _li, "act")
            evq[i] = b.n["act"]


def _emit_wprojT(nc, b):
    PS = b.PS
    with nc.Block() as blk:
        _phase_gate(nc, blk, b)
        evq = {}
        for i in range(8):
            stg = b.arena[6][:, (i % 2) * 1024:(i % 2 + 1) * 1024]
            @blk.sync
            def _(sy, i=i, stg=stg):
                if i >= 2:
                    sy.wait_ge(b.s_act, evq[i - 2])
                b.lw = _pload(b, sy, stg, b.wproj_d[i * 128:(i + 1) * 128, :], i % 2)
            lw = b.lw

            @blk.tensor
            def _(t, i=i, stg=stg, lw=lw):
                t.wait_ge(*lw)
                if i >= 1:
                    t.wait_ge(b.s_act, evq[i - 1])
                for k in range(KC):
                    p = PS[k // 4][:, (k % 4) * 128:(k % 4 + 1) * 128]
                    _li = t.transpose(p, stg[:, k * 128:(k + 1) * 128], b.idf[:])
                _i(b, _li, "pe")
            npe = b.n["pe"]

            @blk.scalar
            def _(s, i=i, npe=npe):
                s = _Ch(b, s, "act")
                s.wait_ge(b.s_pe, npe)
                for k in range(KC):
                    p = PS[k // 4][:, (k % 4) * 128:(k % 4 + 1) * 128]
                    _li = s.copy(b.wp_t[k][:, i * 128:(i + 1) * 128], p)
                _i(b, _li, "act")
            evq[i] = b.n["act"]


def _w_src(b, i):
    if i < 32:
        return b.w1_d[i * 128:(i + 1) * 128, :]
    j = i - 32
    oc, q = j % 8, j // 8
    return b.w2_d[oc * 128:(oc + 1) * 128, q * 1024:(q + 1) * 1024]


def _emit_ffn_quant(nc, b):
    PS = b.PS
    with nc.Block() as blk:
        _phase_gate(nc, blk, b)
        evq = {}
        for i in range(64):
            @blk.sync
            def _(sy, i=i):
                if i >= 2:
                    sy.wait_ge(b.s_act, evq[i - 2])
                b.lw = _pload(b, sy, b.wp_t[i % 2][:], _w_src(b, i), i % 2)
            lw = b.lw

            @blk.scalar
            def _(s, i=i, lw=lw):
                s = _Ch(b, s, "act")
                s.wait_ge(*lw)
                _i(b, s.activation(b.wp_t[2][:], b.wp_t[i % 2][:], AF.Abs,
                                   accum_out=b.asum[:, i:i + 1]), "act")
            evq[i] = b.n["act"]

        @blk.vector
        def _(v):
            v = _Ch(b, v, "dve")
            v.wait_ge(b.s_act, b.n["act"])
            v.reduce_sum(b.tmp[:, 0:1], b.asum[:, 0:32], axis=AX.X)
            _i(b, v.reduce_sum(b.tmp[:, 1:2], b.asum[:, 32:64], axis=AX.X), "dve")
        nd = b.n["dve"]

        @blk.tensor
        def _(t, nd=nd):
            t.wait_ge(b.s_dve, nd)
            t.matmul(PS[2][0:1, 0:1], b.tmp[:, 0:1], b.ones_col[:], start=True, stop=True)
            _i(b, t.matmul(PS[2][0:1, 1:2], b.tmp[:, 1:2], b.ones_col[:],
                           start=True, stop=True), "pe")
        np1 = b.n["pe"]

        @blk.vector
        def _(v, np1=np1):
            v = _Ch(b, v, "dve")
            v.wait_ge(b.s_pe, np1)
            v.tensor_scalar(b.sml[0:1, 0:2], PS[2][0:1, 0:2], 1.0 / (HID * C), 1e-5,
                            ALU.mult, ALU.max)
            _i(b, v.reciprocal(b.sml[0:1, 2:4], b.sml[0:1, 0:2]), "dve")
        nd2 = b.n["dve"]

        @blk.tensor
        def _(t, nd2=nd2):
            t.wait_ge(b.s_dve, nd2)
            t.matmul(PS[2][:, 8:9], b.ones_row[:], b.sml[0:1, 2:3], start=True, stop=True)
            t.matmul(PS[2][:, 9:10], b.ones_row[:], b.sml[0:1, 3:4], start=True, stop=True)
            t.matmul(PS[2][:, 10:11], b.ones_row[:], b.sml[0:1, 0:1], start=True, stop=True)
            _i(b, t.matmul(PS[2][:, 11:12], b.ones_row[:], b.sml[0:1, 1:2],
                           start=True, stop=True), "pe")
        np2 = b.n["pe"]

        @blk.vector
        def _(v, np2=np2):
            v = _Ch(b, v, "dve")
            v.wait_ge(b.s_pe, np2)
            _i(b, v.tensor_copy(b.wsc[:, 0:4], PS[2][:, 8:12]), "dve")

        if "wsc" in b.tapd:
            ndv = b.n["dve"]
            @blk.gpsimd
            def _(g, ndv=ndv):
                g.wait_ge(b.s_dve, ndv)
                g.wait_ge(b.s_st, b.n["st"])
                _i(b, g.dma_start(out=b.tapd["wsc"], in_=b.wsc[:]), "st", 16)

    i32v = b.wp_t[3][:, :].bitcast(I32)
    bfv = b.wp_t[4][:, :].bitcast(BF16)[:, 0:1024]
    with nc.Block() as blk:
        _phase_gate(nc, blk, b)
        evq = {}
        for i in range(64):
            @blk.sync
            def _(sy, i=i):
                if i >= 2:
                    sy.wait_ge(b.s_act, evq[i - 2])
                b.lw = _pload(b, sy, b.wp_t[i % 2][:], _w_src(b, i), i % 2)
            lw = b.lw

            @blk.vector
            def _(v, i=i, lw=lw):
                v = _Ch(b, v, "dve")
                v.wait_ge(*lw)
                if i >= 1:
                    v.wait_ge(b.s_pe, evp)
                wscol = b.wsc[:, 0:1] if i < 32 else b.wsc[:, 1:2]
                w5 = b.wp_t[5][:, :]
                v.tensor_scalar(w5, b.wp_t[i % 2][:], wscol, -1.499,
                                ALU.mult, ALU.max)
                v.tensor_scalar_min(w5, w5, 1.499)
                _i(b, _round_robust(v, b, w5, i32v, b.wp_t[6][:, :],
                                    b.wp_t[7][:, :], bfv), "dve")
            ndv = b.n["dve"]

            @blk.tensor
            def _(t, i=i, ndv=ndv):
                t.wait_ge(b.s_dve, ndv)
                if i >= 1:
                    t.wait_ge(b.s_act, evq[i - 1])
                for k in range(KC):
                    p = PS[5][:, k * 64:(k + 1) * 64].bitcast(BF16)
                    _li = t.transpose(p, bfv[:, k * 128:(k + 1) * 128], b.idb[:])
                _i(b, _li, "pe")
            evp = b.n["pe"]

            @blk.scalar
            def _(s, i=i, evp=evp):
                s = _Ch(b, s, "act")
                s.wait_ge(b.s_pe, evp)
                for k in range(KC):
                    p = PS[5][:, k * 64:(k + 1) * 64].bitcast(BF16)
                    if i < 32:
                        dest = b.w1q[k][:, i * 128:(i + 1) * 128]
                    else:
                        j = i - 32
                        oc, q = j % 8, j // 8
                        dest = b.w2q[q * 8 + k][:, oc * 128:(oc + 1) * 128]
                    _li = s.copy(dest, p)
                _i(b, _li, "act")
            evq[i] = b.n["act"]

        if "w1q" in b.tapd:
            na = b.n["act"]
            @blk.gpsimd
            def _(g, na=na):
                g.wait_ge(b.s_act, na)
                g.wait_ge(b.s_st, b.n["st"])
                _i(b, g.dma_start(out=b.tapd["w1q"], in_=b.w1q[0][:, :]), "st", 16)
        if "w2q" in b.tapd:
            na = b.n["act"]
            @blk.gpsimd
            def _(g, na=na):
                g.wait_ge(b.s_act, na)
                g.wait_ge(b.s_st, b.n["st"])
                _i(b, g.dma_start(out=b.tapd["w2q"], in_=b.w2q[0][:, :]), "st", 16)


def _ln_small_ops(b, v, mcol, rcol):
    tm = b.tmp
    v.tensor_scalar_mul(tm[:, mcol:mcol + 1], tm[:, mcol:mcol + 1], 1.0 / C)
    v.tensor_scalar_mul(tm[:, 9:10], tm[:, 8:9], 1.0 / C)
    v.tensor_tensor(tm[:, 10:11], tm[:, mcol:mcol + 1], tm[:, mcol:mcol + 1], ALU.mult)
    v.scalar_tensor_tensor(tm[:, 11:12], tm[:, 10:11], -1.0, tm[:, 9:10],
                           ALU.mult, ALU.add)
    v.tensor_scalar_add(tm[:, 11:12], tm[:, 11:12], EPS_LN)
    _newton_rsqrt(v, tm[:, rcol:rcol + 1], tm[:, 11:12], tm[:, 12:13], tm[:, 13:14])
    return v.tensor_scalar(tm[:, mcol + 1:mcol + 2], tm[:, mcol:mcol + 1],
                           tm[:, rcol:rcol + 1], -1.0, ALU.mult, ALU.mult)


def _emit_A(nc, b, s):
    PS = b.PS
    W = b.wp_t
    xr = [W[0][:, :], W[1][:, :]]
    xntm, xnfm = W[2][:, :], W[3][:, :]
    ak, khat, vsb = W[4][:, :], W[5][:, :], W[6][:, :]
    lns = W[7][:, :]

    with nc.Block() as blk:
        _phase_gate(nc, blk, b)
        done = {}
        for t in range(NT):
            @blk.sync
            def _(sy, t=t):
                if t >= 2:
                    sy.wait_ge(b.s_dve, done[t - 2][0])
                    sy.wait_ge(b.s_act, done[t - 2][1])
                b.lw = _pload(b, sy, xr[t % 2],
                              b.x_d[s * SEQ + t * T:s * SEQ + (t + 1) * T, :], t % 2)
            lw = b.lw

            act0 = b.n["act"]
            prev_xn = getattr(b, "a_prev_xn", 0)
            prev_tr = getattr(b, "a_prev_tr", 0)
            prev_fm = getattr(b, "a_prev_fm", 0)
            prev_ak = getattr(b, "a_prev_ak", 0)
            prev_vev = getattr(b, "a_prev_vev", 0)
            prev_acc = getattr(b, "a_prev_acc", 0)

            @blk.vector
            def _(v, t=t, lw=lw, act0=act0, prev_xn=prev_xn):
                v = _Ch(b, v, "dve")
                v.wait_ge(*lw)
                if t >= 1:
                    v.wait_ge(b.s_act, prev_xn)
                v.reduce_sum(b.tmp[:, 0:1], xr[t % 2], axis=AX.X)
                v.wait_ge(b.s_act, act0 + 1)
                _li = _ln_small_ops(b, v, 0, 2)
                _i(b, _li, "dve")
            nsum = b.n["dve"]
            nln = b.n["dve"]

            @blk.tensor
            def _(tt, t=t, act0=act0, prev_fm=prev_fm, prev_ak=prev_ak,
                  prev_vev=prev_vev):
                tt.wait_ge(b.s_act, act0 + 2)
                if t >= 1:
                    tt.wait_ge(b.s_act, prev_fm)
                for k in range(KC):
                    p = PS[k // 4][:, (k % 4) * 128:(k % 4 + 1) * 128]
                    _li = tt.transpose(p, xntm[:, k * 128:(k + 1) * 128], b.idf[:])
                _i(b, _li, "pe")
                tt.wait_ge(b.s_act, act0 + 4)
                if t >= 1:
                    tt.wait_ge(b.s_dve, prev_ak)
                    tt.wait_ge(b.s_act, prev_vev)
                for f in range(4):
                    for k in range(KC):
                        _li = tt.matmul(PS[2 + f][:], xnfm[:, k * 128:(k + 1) * 128],
                                  b.arena[k][:, f * 512:(f + 1) * 512],
                                  start=(k == 0), stop=(k == KC - 1))
                _i(b, _li, "pe")
            ntr = b.n["pe"] - 1
            nkv = b.n["pe"]
            b.a_prev_tr = ntr

            @blk.vector
            def _(v, nkv=nkv, t=t, prev_acc=prev_acc):
                v = _Ch(b, v, "dve")
                v.wait_ge(b.s_pe, nkv)
                if t >= 1:
                    v.wait_ge(b.s_pe, prev_acc)
                v.tensor_scalar(ak[:, 0:512], PS[2][:], 0.0, EPS, ALU.max, ALU.add)
                v.tensor_scalar(ak[:, 512:1024], PS[3][:], 0.0, EPS, ALU.max, ALU.add)
                v.tensor_tensor(khat, ak, ak, ALU.mult)
                v.tensor_tensor(khat, khat, ak, ALU.mult)
                akv = khat.rearrange("p (h d) -> p h d", d=D)
                v.reduce_sum(b.tmp[:, 16:32], akv, axis=AX.X)
                v.reciprocal(b.tmp[:, 16:32], b.tmp[:, 16:32])
                rb = b.tmp[:, 16:32].unsqueeze(2).broadcast_to([128, 16, D])
                _i(b, v.tensor_tensor(akv, akv, rb, ALU.mult), "dve")
            nak = b.n["dve"]
            b.a_prev_ak = nak

            @blk.tensor
            def _(tt, t=t, nak=nak, act0=act0):
                tt.wait_ge(b.s_dve, nak)
                tt.wait_ge(b.s_act, act0 + 6)
                for h in range(H):
                    c, r = h // 2, (h % 2) * 64
                    tt.matmul(PS[6][r:r + 64, c * 64:(c + 1) * 64],
                              khat[:, h * D:(h + 1) * D], vsb[:, h * D:(h + 1) * D],
                              start=(t == 0 and h < 2),
                              stop=(t == NT - 1 and h >= H - 2),
                              tile_position=(0, r), skip_group_check=True)
                for h in range(H):
                    c, r = h // 2, (h % 2) * 64
                    _li = tt.matmul(PS[7][r:r + 64, c:c + 1],
                              khat[:, h * D:(h + 1) * D], b.ones_col[:],
                              start=(t == 0 and h < 2),
                              stop=(t == NT - 1 and h >= H - 2),
                              tile_position=(0, r), skip_group_check=True)
                _i(b, _li, "pe")
            b.a_prev_acc = b.n["pe"]

            @blk.scalar
            def _(sc, t=t, lw=lw, nln=nln, ntr=ntr, nkv=nkv, act0=act0,
                  prev_tr=prev_tr, prev_acc=prev_acc):
                sc = _Ch(b, sc, "act")
                sc.wait_ge(*lw)
                if t >= 1:
                    sc.wait_ge(b.s_pe, prev_tr)
                sc.activation(lns, xr[t % 2], AF.Square,
                              accum_out=b.tmp[:, 8:9])
                sc.wait_ge(b.s_dve, nln)
                sc.activation(xntm, xr[t % 2], AF.Identity,
                              bias=b.tmp[:, 1:2], scale=b.tmp[:, 2:3])
                sc.wait_ge(b.s_pe, ntr)
                sc.copy(xnfm[:, 0:512], PS[0][:])
                sc.copy(xnfm[:, 512:1024], PS[1][:])
                sc.wait_ge(b.s_pe, nkv)
                if t >= 1:
                    sc.wait_ge(b.s_pe, prev_acc)
                sc.copy(vsb[:, 0:512], PS[4][:])
                _i(b, sc.copy(vsb[:, 512:1024], PS[5][:]), "act")
            assert b.n["act"] == act0 + 6
            nxn = act0 + 2
            nvev = b.n["act"]
            b.a_prev_xn = nxn
            b.a_prev_fm = act0 + 4
            b.a_prev_vev = nvev
            done[t] = (nak, nvev)

        nacc = b.n["pe"]
        @blk.vector
        def _(v, nacc=nacc, s=s):
            v = _Ch(b, v, "dve")
            v.wait_ge(b.s_pe, nacc)
            v.tensor_copy(b.kv_sb[s][:], PS[6][:])
            v.memset(b.zmask[s][:], 0.0)
            for h in range(H):
                c, r = h // 2, (h % 2) * 64
                _li = v.tensor_copy(b.zmask[s][r:r + 64, h:h + 1], PS[7][r:r + 64, c:c + 1])
            _i(b, _li, "dve")

        if s == 0 and ("kv0" in b.tapd or "zmask0" in b.tapd):
            ndv = b.n["dve"]
            @blk.gpsimd
            def _(g, ndv=ndv):
                g.wait_ge(b.s_dve, ndv)
                g.wait_ge(b.s_st, b.n["st"])
                if "kv0" in b.tapd:
                    _i(b, g.dma_start(out=b.tapd["kv0"], in_=b.kv_sb[0][:]), "st", 16)
                if "zmask0" in b.tapd:
                    _i(b, g.dma_start(out=b.tapd["zmask0"], in_=b.zmask[0][:]), "st", 16)


_QOC = [0, 4, 1, 5, 2, 6, 3, 7]
_BC_STEPS = {0: NT + 2, 1: NT + 2}


def _emit_BC(nc, b, s):
    nsteps = _BC_STEPS.get(s, NT + 2)
    PS = b.PS
    tapt = 3
    dotap = (s == 0)

    with nc.Block() as blk:
        _phase_gate(nc, blk, b)
        if s == 0:
            @blk.vector
            def _(v):
                v = _Ch(b, v, "dve")
                for w in range(2):
                    pad = b.xqw[w].rearrange("p (c w) -> p c w", w=132)[:, :, 130:132]
                    v.memset(pad, 0.0)
        b.bc_scr_free = b.n["st"]
        b.bc_x_used = {}
        b.bc_x1_used = {}
        b.bc_ow_ok = {}
        b.bc_xw_ok = {}
        b.bc_s4 = {}
        b.bc_s6 = {}

        for step in range(nsteps):
            t = step
            if t < NT:
                _bc_S1(nc, blk, b, s, t)
                _bc_S2(nc, blk, b, s, t, dotap and t == tapt)
                _bc_S3(nc, blk, b, s, t, dotap and t == tapt)
            if 1 <= step <= NT:
                _bc_S4(nc, blk, b, s, step - 1, dotap and step - 1 == tapt)
                _bc_S5(nc, blk, b, s, step - 1, dotap and step - 1 == tapt)
            if 2 <= step <= NT + 1:
                _bc_S6(nc, blk, b, s, step - 2, dotap and step - 2 == tapt)
                _bc_S7(nc, blk, b, s, step - 2)


def _ow3(b, t):
    return b.outw[t % 2].rearrange("p (c w) -> p c w", w=130)


def _xw3(b, t):
    return b.xqw[t % 2].rearrange("p (c w) -> p c w", w=132)


def _cwin(b, t):
    return b.cw_win[0:1, (t % 2) * 132:(t % 2) * 132 + 132]


def _bc_S1(nc, blk, b, s, t):
    @blk.sync
    def _(sy, t=t):
        if t >= 2:
            for sem, val in b.bc_x_used.get(t - 2, ()):
                sy.wait_ge(sem, val)
        b.bc_lw = _pload(b, sy, b.xr[t % 2],
                         b.x_d[s * SEQ + t * T:s * SEQ + (t + 1) * T, :], t % 2)


def _bc_S2(nc, blk, b, s, t, tap):
    PS = b.PS
    lw = b.bc_lw

    act0 = b.n["act"]
    pe0 = b.n["pe"]

    @blk.vector
    def _(v, t=t, lw=lw, act0=act0):
        v = _Ch(b, v, "dve")
        v.wait_ge(*lw)
        v.reduce_sum(b.tmp[:, 0:1], b.xr[t % 2], axis=AX.X)
        v.wait_ge(b.s_act, act0 + 1)
        _li = _ln_small_ops(b, v, 0, 2)
        _i(b, _li, "dve")
    nsum = b.n["dve"]
    nln = b.n["dve"]
    dve0q = b.n["dve"]

    @blk.tensor
    def _(tt, act0=act0):
        tt.wait_ge(b.s_act, act0 + 2)
        for k in range(KC):
            p = PS[k // 4][:, (k % 4) * 128:(k % 4 + 1) * 128]
            _li = tt.transpose(p, b.xntm[:, k * 128:(k + 1) * 128], b.idf[:])
        _i(b, _li, "pe")
        tt.wait_ge(b.s_act, act0 + 4)
        for idx, oc in enumerate(_QOC):
            if idx >= 2:
                tt.wait_ge(b.s_dve, dve0q + 3 * (idx - 1))
            p = PS[oc // 4][:, (oc % 4) * 128:(oc % 4 + 1) * 128]
            for k in range(KC):
                _li = tt.matmul(p, b.wq_t[k][:, oc * 128:(oc + 1) * 128],
                          b.xnfm[:, k * 128:(k + 1) * 128],
                          start=(k == 0), stop=(k == KC - 1))
            _i(b, _li, "pe")
    assert b.n["pe"] == pe0 + 1 + KC
    ntr = pe0 + 1
    nq = b.n["pe"]

    @blk.scalar
    def _(sc, t=t, lw=lw, nln=nln, ntr=ntr):
        sc = _Ch(b, sc, "act")
        sc.wait_ge(*lw)
        sc.wait_ge(b.s_st, b.bc_scr_free)
        sc.wait_ge(b.s_pe, getattr(b, "bc_zmm", 0))
        sc.activation(b.scratch, b.xr[t % 2], AF.Square,
                      accum_out=b.tmp[:, 8:9])
        sc.wait_ge(b.s_dve, nln)
        sc.activation(b.xntm, b.xr[t % 2], AF.Identity,
                      bias=b.tmp[:, 1:2], scale=b.tmp[:, 2:3])
        sc.wait_ge(b.s_pe, ntr)
        sc.copy(b.xnfm[:, 0:512], PS[0][:])
        _li = sc.copy(b.xnfm[:, 512:1024], PS[1][:])
        _i(b, _li, "act")
    assert b.n["act"] == act0 + 4
    nxn = act0 + 2
    nfm = b.n["act"]
    if tap:
        _tap(nc, blk, b, "xnfm", b.xnfm, [(b.s_act, nfm)])

    b.bc_q3c = {}
    @blk.vector
    def _(v, ntr=ntr):
        v = _Ch(b, v, "dve")
        for idx, oc in enumerate(_QOC):
            v.wait_ge(b.s_pe, ntr + 1 + idx)
            p = PS[oc // 4][:, (oc % 4) * 128:(oc % 4 + 1) * 128]
            qc = b.q3[:, oc * 128:(oc + 1) * 128]
            v.tensor_scalar(qc, p, 0.0, EPS, ALU.max, ALU.add)
            v.tensor_tensor(b.mixc, qc, qc, ALU.mult)
            v.tensor_tensor(qc, b.mixc, qc, ALU.mult)
            b.bc_q3c[oc] = b.n["dve"]
    assert b.n["dve"] == dve0q + 24
    b.bc_q3 = b.n["dve"]
    if tap:
        _tap(nc, blk, b, "q3", b.q3, [(b.s_dve, b.bc_q3)])

    b.bc_x_used.setdefault(t, []).append((b.s_dve, nsum))
    b.bc_x_used[t].append((b.s_act, nxn))


def _bc_S3(nc, blk, b, s, t, tap):
    PS = b.PS
    nq3 = b.bc_q3

    @blk.tensor
    def _(tt, t=t, s=s):
        for c in range(KC):
            tt.wait_ge(b.s_dve, b.bc_q3c[c])
            p = PS[c // 4][:, (c % 4) * 128:(c % 4 + 1) * 128]
            q3b = b.q3[:, c * 128:(c + 1) * 128]
            kvc = b.kv_sb[s][:, c * 64:(c + 1) * 64]
            tt.matmul(p[0:64, :], kvc[0:64, :], q3b[0:64, :],
                      start=True, stop=True, tile_position=(0, 0))
            _li = tt.matmul(p[64:128, :], kvc[64:128, :], q3b[64:128, :],
                      start=True, stop=True, tile_position=(64, 64))
        _i(b, _li, "pe")
    nout = b.n["pe"]

    pe0z = b.n["pe"]
    zd0 = b.n["dve"]
    zdve_p = {c: zd0 + 3 * (c + 1) for c in range(KC)}
    nz_p = {c: pe0z + (1 if c == 0 else 2 * c) for c in range(KC)}
    zb_p = {c: pe0z + (2 * c + 3 if c < KC - 1 else 16) for c in range(KC)}
    zsl = [b.zd[0:2, 128:256], b.zd[0:2, 256:384]]

    @blk.tensor
    def _(tt, s=s):
        for c in range(KC):
            bank = PS[2] if c % 2 == 0 else PS[3]
            if c >= 2:
                tt.wait_ge(b.s_dve, zdve_p[c - 2])
            tt.matmul(bank[0:2, 0:128], b.zmask[s][:, 2 * c:2 * c + 2],
                      b.q3[:, c * 128:(c + 1) * 128], start=True, stop=True)
            _li = tt.matmul(bank[0:2, 128:256], b.onesh[:],
                      b.q3[:, c * 128:(c + 1) * 128], start=True, stop=True)
            _i(b, _li, "pe")
            if c >= 1:
                tt.wait_ge(b.s_dve, zdve_p[c - 1])
                zp = PS[4] if c - 1 < 4 else PS[5]
                _i(b, tt.matmul(zp[:, ((c - 1) % 4) * 128:((c - 1) % 4 + 1) * 128],
                                b.mask2[:], zsl[(c - 1) % 2],
                                start=True, stop=True), "pe")
        tt.wait_ge(b.s_dve, zdve_p[KC - 1])
        _i(b, tt.matmul(PS[5][:, 384:512], b.mask2[:], zsl[(KC - 1) % 2],
                        start=True, stop=True), "pe")
    assert b.n["pe"] == pe0z + 16
    b.bc_zdve = dict(zdve_p)

    @blk.vector
    def _(v):
        v = _Ch(b, v, "dve")
        for c in range(KC):
            bank = PS[2] if c % 2 == 0 else PS[3]
            v.wait_ge(b.s_pe, nz_p[c])
            if c >= 2:
                v.wait_ge(b.s_pe, zb_p[c - 2])
            v.tensor_copy(b.zd[0:2, 0:128], bank[0:2, 0:128])
            v.scalar_tensor_tensor(zsl[c % 2], bank[0:2, 128:256], EPS,
                                   b.zd[0:2, 0:128], ALU.mult, ALU.add)
            v.reciprocal(zsl[c % 2], zsl[c % 2])
    assert b.n["dve"] == zd0 + 24
    nzb = pe0z + 16
    b.bc_zmm = nzb

    ow = _ow3(b, t)
    owm1 = _ow3(b, t - 1)

    @blk.scalar
    def _(sc, nout=nout, t=t):
        sc = _Ch(b, sc, "act")
        sc.wait_ge(b.s_pe, nout)
        if t >= 1:
            sc.wait_ge(b.s_dve, b.bc_s4.get(t - 2, 0))
        sc.copy(ow[:, 0:4, 1:129], PS[0][:].rearrange("p (c w) -> p c w", w=128))
        _i(b, sc.copy(ow[:, 4:8, 1:129],
                      PS[1][:].rearrange("p (c w) -> p c w", w=128)), "act")
    nev = b.n["act"]

    @blk.vector
    def _(v, nev=nev, nzb=nzb, t=t):
        v = _Ch(b, v, "dve")
        v.wait_ge(b.s_act, nev)
        v.wait_ge(b.s_pe, nzb)
        v.tensor_tensor(ow[:, 0:4, 1:129], ow[:, 0:4, 1:129],
                        PS[4][:].rearrange("p (c w) -> p c w", w=128), ALU.mult)
        v.tensor_tensor(ow[:, 4:8, 1:129], ow[:, 4:8, 1:129],
                        PS[5][:].rearrange("p (c w) -> p c w", w=128), ALU.mult)
        if t == 0:
            v.memset(ow[:, :, 0:1], 0.0)
        else:
            v.tensor_copy(ow[:, :, 0:1], owm1[:, :, 128:129])
        if t >= 1:
            v.tensor_copy(owm1[:, :, 129:130], ow[:, :, 1:2])
    b.bc_ow_ok[t] = b.n["dve"]
    if t == NT - 1:
        @blk.vector
        def _(v):
            v = _Ch(b, v, "dve")
            _i(b, v.memset(ow[:, :, 129:130], 0.0), "dve")
        b.bc_ow_ok[t] = b.n["dve"]


def _bc_S4(nc, blk, b, s, t, tap):
    PS = b.PS
    ow = _ow3(b, t)
    need = b.bc_ow_ok.get(t + 1, b.bc_ow_ok[t])

    if tap:
        _tap(nc, blk, b, "outw", b.outw[t % 2], [(b.s_dve, need)])

    act0 = b.n["act"]
    pe0 = b.n["pe"]
    dve0 = b.n["dve"]
    prj_p = {c: pe0 + c + 1 for c in range(KC)}
    nd_p = {c: dve0 + 3 * (c + 1) for c in range(KC)}

    if not hasattr(b, "bc_prj"):
        b.bc_prj = {}

    @blk.tensor
    def _(tt, t=t):
        for c in range(KC):
            tt.wait_ge(b.s_dve, nd_p[c])
            if c == 0:
                tt.wait_ge(b.s_dve, b.bc_x1_used.get(t - 2, 0))
            mA = b.mixa if c % 2 == 0 else b.mixb
            tt.matmul(PS[6][:], mA, b.wp_t[c][:, 0:512],
                      start=(c == 0), stop=(c == KC - 1))
            _li = tt.matmul(PS[7][:], mA, b.wp_t[c][:, 512:1024],
                      start=(c == 0), stop=(c == KC - 1))
            _i(b, _li, "pe")
    assert b.n["pe"] == pe0 + KC
    for c in range(KC):
        b.bc_prj[c] = prj_p[c]
    nprj = b.n["pe"]

    x1 = b.x1r[t % 2]
    @blk.vector
    def _(v, nprj=nprj, need=need, t=t):
        v = _Ch(b, v, "dve")
        v.wait_ge(b.s_dve, need)
        v.wait_ge(b.s_act, b.bc_silu.get(JC - 1, 0) if hasattr(b, "bc_silu")
                  else 0)
        for c in range(KC):
            mA = b.mixa if c % 2 == 0 else b.mixb
            mB = b.mixc if c % 2 == 0 else b.mixd
            if c >= 2:
                v.wait_ge(b.s_pe, prj_p[c - 2])
            v.scalar_tensor_tensor(mB, ow[:, c, 0:128], b.dwc_w0[:, c:c + 1],
                                   b.bdwc[:, c:c + 1].broadcast_to([128, 128]),
                                   ALU.mult, ALU.add)
            v.scalar_tensor_tensor(mA, ow[:, c, 1:129], b.dwc_w1p[:, c:c + 1],
                                   mB, ALU.mult, ALU.add)
            v.scalar_tensor_tensor(mA, ow[:, c, 2:130],
                                   b.dwc_w2[:, c:c + 1], mA,
                                   ALU.mult, ALU.add)
        v.wait_ge(b.s_pe, nprj)
        v.scalar_tensor_tensor(x1[:, 0:512], PS[6][:], 1.0,
                               b.bproj_bc[:, 0:512], ALU.mult, ALU.add)
        v.scalar_tensor_tensor(x1[:, 512:1024], PS[7][:], 1.0,
                               b.bproj_bc[:, 512:1024], ALU.mult, ALU.add)
        _i(b, v.tensor_tensor(x1, x1, b.xr[t % 2], ALU.add), "dve")
    assert b.n["dve"] == dve0 + 3 * KC + 3
    b.bc_x1 = b.n["dve"]
    b.bc_s4[t] = b.n["dve"]
    b.bc_x_used.setdefault(t, []).append((b.s_dve, b.n["dve"]))
    if tap:
        _tap(nc, blk, b, "x1", x1, [(b.s_dve, b.bc_x1)])


def _bc_S5(nc, blk, b, s, t, tap):
    PS = b.PS
    x1 = b.x1r[t % 2]
    nx1 = b.bc_x1
    tm = b.tmp

    act0 = b.n["act"]

    @blk.vector
    def _(v, nx1=nx1, act0=act0):
        v = _Ch(b, v, "dve")
        v.wait_ge(b.s_dve, nx1)
        v.reduce_sum(tm[:, 4:5], x1, axis=AX.X)
        v.wait_ge(b.s_act, act0 + 1)
        v.tensor_scalar_mul(tm[:, 4:5], tm[:, 4:5], 1.0 / C)
        v.tensor_scalar_mul(tm[:, 9:10], tm[:, 8:9], 1.0 / C)
        v.tensor_tensor(tm[:, 10:11], tm[:, 4:5], tm[:, 4:5], ALU.mult)
        v.scalar_tensor_tensor(tm[:, 11:12], tm[:, 10:11], -1.0, tm[:, 9:10],
                               ALU.mult, ALU.add)
        v.tensor_scalar_add(tm[:, 11:12], tm[:, 11:12], EPS_LN)
        _newton_rsqrt(v, tm[:, 6:7], tm[:, 11:12], tm[:, 12:13], tm[:, 13:14])
        v.tensor_scalar_mul(tm[:, 5:6], tm[:, 4:5], -1.0)
        b.bc_nm2 = b.n["dve"]
        v.wait_ge(b.s_act, act0 + 2)
        v.reduce_max(tm[:, 7:8], b.scratch, axis=AX.X, apply_absolute_value=True)
        v.tensor_tensor(tm[:, 15:16], tm[:, 6:7], tm[:, 7:8], ALU.mult)
        v.tensor_scalar_max(tm[:, 16:17], tm[:, 15:16], 1e-5)
        v.reciprocal(tm[:, 14:15], tm[:, 16:17])
        v.tensor_scalar(tm[:, 14:15], tm[:, 14:15], tm[:, 6:7], 127.0,
                        ALU.mult, ALU.mult)
        _li = v.tensor_scalar(tm[:, 17:18], tm[:, 16:17], b.wsc[:, 2:3], 1.0 / 127.0,
                        ALU.mult, ALU.mult)
        _i(b, _li, "dve")
    nm2 = b.bc_nm2
    nct = b.n["dve"]

    ntp = None
    for hh in range(2):
        @blk.vector
        def _(v, hh=hh, nct=nct):
            v = _Ch(b, v, "dve")
            if hh == 1:
                v.wait_ge(b.s_pe, b.bc_s5h)
            sh = b.scratch[:, hh * 512:(hh + 1) * 512]
            v.tensor_scalar_mul(sh, sh, tm[:, 14:15])
            _i(b, _round_robust(v, b, sh, b.i32b[:, :], PS[6][:, 0:512],
                                b.xqtm[:, :], b.xqtm[:, :]), "dve")
        nqz = b.n["dve"]

        @blk.tensor
        def _(tt, hh=hh, nqz=nqz, t=t):
            tt.wait_ge(b.s_dve, nqz)
            if hh == 0 and t >= 1:
                tt.wait_ge(b.s_act, b.bc_xw_ok.get(t - 1, 0))
            p2b = PS[2][:, :].bitcast(BF16)
            for k in range(4):
                _li = tt.transpose(p2b[:, (hh * 4 + k) * 128:(hh * 4 + k + 1) * 128],
                                   b.xqtm[:, k * 128:(k + 1) * 128], b.idb[:])
            if hh == 1:
                _li = tt.transpose(PS[3][0:1, 0:128], tm[:, 17:18], b.idf[:])
            _i(b, _li, "pe")
        b.bc_s5h = b.n["pe"]
    ntp = b.n["pe"]
    b.bc_s5qz = b.n["dve"]

    xw = _xw3(b, t)
    xwm1 = _xw3(b, t - 1)
    cw = _cwin(b, t)
    cwm1 = _cwin(b, t - 1)

    @blk.scalar
    def _(sc, nx1=nx1, nm2=nm2, ntp=ntp, t=t):
        sc = _Ch(b, sc, "act")
        sc.wait_ge(b.s_dve, nx1)
        sc.wait_ge(b.s_dve, getattr(b, 'bc_mix', b.bc_q3))
        sc.wait_ge(b.s_pe, b.bc_zmm)
        sc.wait_ge(b.s_st, b.bc_scr_free)
        sc.activation(b.scratch, x1, AF.Square, accum_out=tm[:, 8:9])
        sc.wait_ge(b.s_dve, nm2)
        sc.activation(b.scratch, x1, AF.Identity, bias=tm[:, 5:6], scale=1.0)
        sc.wait_ge(b.s_pe, ntp)
        if t >= 1:
            sc.wait_ge(b.s_dve, b.bc_s6.get(t - 2, 0))
        p2b = PS[2][:, :].bitcast(BF16).rearrange("p (c w) -> p c w", w=128)
        sc.copy(xw[:, :, 1:129], p2b)
        _i(b, sc.copy(cw[0:1, 1:129], PS[3][0:1, 0:128]), "act")
    assert b.n["act"] == act0 + 4
    ncp = b.n["act"]

    @blk.vector
    def _(v, ncp=ncp, t=t):
        v = _Ch(b, v, "dve")
        v.wait_ge(b.s_act, ncp)
        if t == 0:
            v.memset(xw[:, :, 0:1], 0.0)
            v.memset(cw[0:1, 0:1], 1.0)
        else:
            v.tensor_copy(xw[:, :, 0:1], xwm1[:, :, 128:129])
            v.tensor_copy(cw[0:1, 0:1], cwm1[0:1, 128:129])
            v.tensor_copy(xwm1[:, :, 129:130], xw[:, :, 1:2])
            v.tensor_copy(cwm1[0:1, 129:130], cw[0:1, 1:2])
        if t == NT - 1:
            v.memset(xw[:, :, 129:130], 0.0)
            v.memset(cw[0:1, 129:130], 1.0)
    b.bc_xw_ok[t] = b.n["act"]
    b.bc_xw_dve = b.n["dve"]
    if tap:
        _tap(nc, blk, b, "xq", b.xqw[t % 2], [(b.s_dve, b.bc_xw_dve)])


def _bc_S6(nc, blk, b, s, t, tap):
    PS = b.PS
    xw = _xw3(b, t)
    cw = _cwin(b, t)
    tm = b.tmp
    ndep = b.bc_xw_dve

    @blk.tensor
    def _(tt, ndep=ndep):
        tt.wait_ge(b.s_dve, ndep)
        _i(b, tt.matmul(PS[3][:, 128:258], b.ones_row[:], cw[0:1, 0:130],
                        start=True, stop=True), "pe")
    ncb = b.n["pe"]

    @blk.scalar
    def _(sc, ncb=ncb):
        sc = _Ch(b, sc, "act")
        sc.wait_ge(b.s_pe, ncb)
        _i(b, sc.copy(b.cbc, PS[3][:, 128:258]), "act")
    ncbc = b.n["act"]
    if tap:
        _tap(nc, blk, b, "cbc", b.cbc, [(b.s_act, ncbc)])

    act0 = b.n["act"]
    dve0 = b.n["dve"]
    silu_p = {j: act0 + j + 1 for j in range(JC)}
    hevac_p = {j: dve0 + 1 + 4 * j for j in range(JC)}
    cvC_p = {j: dve0 + 4 * (j + 1) for j in range(JC)}

    nh = {}
    @blk.tensor
    def _(tt, ncbc=ncbc):
        for jc in range(JC):
            hp = PS[2 + jc % 4]
            if jc < 4:
                tt.wait_ge(b.s_act, ncbc)
            else:
                tt.wait_ge(b.s_dve, hevac_p[jc - 4])
            for k in range(KC):
                _li = tt.matmul(hp[:, 0:130], b.w1q[k][:, jc * 128:(jc + 1) * 128],
                                xw[:, k, 0:130], start=(k == 0), stop=(k == KC - 1))
            _i(b, _li, "pe")
            nh[jc] = b.n["pe"]

    def _conv(v, j):
        mA = b.mixa if j % 2 == 0 else b.mixb
        mB = b.mixc if j % 2 == 0 else b.mixd
        if j >= 2:
            v.wait_ge(b.s_act, silu_p[j - 2])
        v.scalar_tensor_tensor(mB, b.s_loc[j][:, 0:128], b.cw0[:, j:j + 1],
                               b.cbf[:, j:j + 1].broadcast_to([128, 128]),
                               ALU.mult, ALU.add)
        v.scalar_tensor_tensor(mA, b.s_loc[j][:, 1:129], b.cw1[:, j:j + 1],
                               mB, ALU.mult, ALU.add)
        v.scalar_tensor_tensor(mA, b.s_loc[j][:, 2:130], b.cw2[:, j:j + 1],
                               mA, ALU.mult, ALU.add)

    sgroups = [(0, 7), (7, 7), (14, 7), (21, 7), (28, 2), (30, 2)]
    @blk.vector
    def _(v):
        v = _Ch(b, v, "dve")
        for jc in range(JC):
            if jc >= 1:
                _conv(v, jc - 1)
            v.wait_ge(b.s_pe, nh[jc])
            v.tensor_tensor(b.s_loc[jc], PS[2 + jc % 4][:, 0:130], b.cbc,
                            ALU.mult)
        _conv(v, JC - 1)
        first = True
        for g0, glen in sgroups:
            base = b.s_loc[g0]
            src = bass.AP(base.tensor, base.offset,
                          [list(base.ap[0]), [1, 128], [130, glen]])
            dst = b.M1 if first else b.mixc
            v.wait_ge(b.s_act, silu_p[g0 + glen - 1])
            v.reduce_max(dst, src, axis=AX.X, apply_absolute_value=True)
            if not first:
                v.tensor_max(b.M1, b.M1, b.mixc)
            first = False
    assert b.n["dve"] == dve0 + 139, b.n["dve"] - dve0
    b.bc_hevac = dict(hevac_p)
    nmx = b.n["dve"]

    @blk.scalar
    def _(sc):
        sc = _Ch(b, sc, "act")
        for jc in range(JC):
            sc.wait_ge(b.s_dve, cvC_p[jc])
            sc.activation(b.s_loc[jc][:, 0:128],
                          b.mixa if jc % 2 == 0 else b.mixb, AF.Silu)
    assert b.n["act"] == act0 + 32, b.n["act"] - act0
    b.bc_silu = dict(silu_p)

    @blk.tensor
    def _(tt, nmx=nmx):
        tt.wait_ge(b.s_dve, nmx)
        _i(b, tt.transpose(PS[3][:, 258:386], b.M1, b.idf[:]), "pe")
    nmt = b.n["pe"]

    @blk.vector
    def _(v, nmt=nmt):
        v = _Ch(b, v, "dve")
        v.wait_ge(b.s_pe, nmt)
        v.reduce_max(tm[:, 20:21], PS[3][:, 258:386], axis=AX.X)
        v.tensor_scalar_max(tm[:, 21:22], tm[:, 20:21], 1e-5)
        v.reciprocal(tm[:, 22:23], tm[:, 21:22])
        v.tensor_scalar_mul(tm[:, 22:23], tm[:, 22:23], 127.0)
        _i(b, v.tensor_scalar(tm[:, 23:24], tm[:, 21:22], b.wsc[:, 3:4],
                              1.0 / 127.0, ALU.mult, ALU.mult), "dve")
    ns2 = b.n["dve"]
    b.bc_c2 = ns2

    @blk.tensor
    def _(tt, ns2=ns2):
        tt.wait_ge(b.s_dve, ns2)
        _i(b, tt.transpose(PS[3][0:1, 0:128], tm[:, 22:23], b.idf[:]), "pe")
    ns2t = b.n["pe"]

    @blk.vector
    def _(v, ns2t=ns2t):
        v = _Ch(b, v, "dve")
        v.wait_ge(b.s_pe, ns2t)
        _i(b, v.tensor_copy(b.zd[0:1, 0:128], PS[3][0:1, 0:128]), "dve")
    ns2r = b.n["dve"]

    @blk.tensor
    def _(tt, ns2r=ns2r):
        tt.wait_ge(b.s_dve, ns2r)
        _i(b, tt.matmul(PS[2][:, 0:128], b.ones_row[:], b.zd[0:1, 0:128],
                        start=True, stop=True), "pe")
    ns2b = b.n["pe"]

    @blk.vector
    def _(v, ns2b=ns2b):
        v = _Ch(b, v, "dve")
        v.wait_ge(b.s_pe, ns2b)
        _i(b, v.tensor_copy(b.s2b, PS[2][:, 0:128]), "dve")
    ns2s = b.n["dve"]

    groups = [(0, 4), (4, 3), (7, 4), (11, 3), (14, 4), (18, 3),
              (21, 4), (25, 3), (28, 2), (30, 2)]
    NG = len(groups)
    pq0 = b.n["pool"]
    dq0 = b.n["dve"]
    pw0 = b.n["pe"]
    dqe_p = {gi: dq0 + 5 * (gi + 1) for gi in range(NG)}
    w2_p = {gi: pw0 + gi + 1 for gi in range(NG)}
    srcs = []
    for g0, glen in groups:
        base = b.s_loc[g0]
        srcs.append(bass.AP(base.tensor, base.offset,
                            [list(base.ap[0]), [130, glen], [1, 128]]))
    if not hasattr(b, "bc_w2"):
        b.bc_w2 = {}

    @blk.gpsimd
    def _(g, ns2s=ns2s):
        g = _Ch(b, g, "pool")
        g.wait_ge(b.s_dve, ns2s)
        for gi, (g0, glen) in enumerate(groups):
            g.tensor_tensor(srcs[gi], srcs[gi],
                            b.s2b.unsqueeze(1).broadcast_to([128, glen, 128]),
                            ALU.mult)
    assert b.n["pool"] == pq0 + NG

    @blk.vector
    def _(v, t=t):
        v = _Ch(b, v, "dve")
        for gi, (g0, glen) in enumerate(groups):
            sqb = b.sq[gi % 2]
            src = srcs[gi]
            v.wait_ge(b.s_pool, pq0 + gi + 1)
            if gi >= 2:
                v.wait_ge(b.s_pe, w2_p[gi - 2])
            else:
                v.wait_ge(b.s_pe, b.bc_w2.get(8 if gi == 0 else 9, 0))
            i32v = b.i32b[:, 0:glen * 128]
            r3 = PS[4][:, 0:glen * 128].rearrange("p (c w) -> p c w", w=128)
            g3 = b.xqtm[:, 0:glen * 128].rearrange("p (c w) -> p c w", w=128)
            i3 = i32v.rearrange("p (c w) -> p c w", w=128)
            v.tensor_copy(i3, src)
            v.tensor_tensor(r3, src, i3, ALU.subtract)
            v.tensor_single_scalar(g3, r3, 0.5, ALU.is_ge)
            v.scalar_tensor_tensor(g3, r3, -0.5, g3, ALU.is_le, ALU.subtract)
            _i(b, v.tensor_tensor(sqb[:, 0:glen * 128], i32v,
                                  b.xqtm[:, 0:glen * 128], ALU.subtract), "dve")
    assert b.n["dve"] == dq0 + 5 * NG

    b.bc_w2 = {}
    @blk.tensor
    def _(tt, t=t):
        for gi, (g0, glen) in enumerate(groups):
            sqb = b.sq[gi % 2]
            tt.wait_ge(b.s_dve, dqe_p[gi])
            if g0 == 0:
                tt.wait_ge(b.s_dve, b.bc_s4.get(t + 1, b.bc_s4.get(t, 0)))
                tt.wait_ge(b.s_dve, b.bc_x1_used.get(t - 1, 0))
            for j in range(g0, g0 + glen):
                lh = sqb[:, (j - g0) * 128:(j - g0 + 1) * 128]
                tt.matmul(PS[6][:], lh, b.w2q[j][:, 0:512],
                          start=(j == 0), stop=(j == JC - 1))
                _li = tt.matmul(PS[7][:], lh, b.w2q[j][:, 512:1024],
                          start=(j == 0), stop=(j == JC - 1))
            _i(b, _li, "pe")
            b.bc_w2[gi] = b.n["pe"]
    assert b.n["pe"] == pw0 + NG
    b.bc_out2 = b.n["pe"]
    b.bc_s6[t] = b.n["pe"]


def _bc_S7(nc, blk, b, s, t):
    PS = b.PS
    tm = b.tmp
    x1 = b.x1r[t % 2]
    nw2 = b.bc_out2

    @blk.scalar
    def _(sc, nw2=nw2, t=t):
        sc = _Ch(b, sc, "act")
        sc.wait_ge(b.s_pe, nw2)
        sc.wait_ge(b.s_dve, max(b.bc_c2, b.bc_s5qz))
        sc.wait_ge(b.s_st, b.bc_scr_free)
        sc.activation(b.x2buf[:, 0:512], PS[6][:], AF.Identity,
                      scale=tm[:, 23:24])
        _i(b, sc.activation(b.x2buf[:, 512:1024], PS[7][:], AF.Identity,
                            scale=tm[:, 23:24]), "act")
    nsc2 = b.n["act"]

    @blk.vector
    def _(v, nsc2=nsc2, t=t):
        v = _Ch(b, v, "dve")
        v.wait_ge(b.s_act, nsc2)
        _i(b, v.tensor_tensor(b.x2buf, b.x2buf, x1, ALU.add), "dve")
    nx2 = b.n["dve"]
    b.bc_x1_used[t] = nx2

    @blk.gpsimd
    def _(g, nx2=nx2, t=t, s=s):
        g.wait_ge(b.s_dve, nx2)
        g.wait_ge(b.s_st, b.n["st"])
        _i(b, g.dma_start(out=b.y_d[s * SEQ + t * T:s * SEQ + (t + 1) * T, :],
                          in_=b.x2buf), "st", 16)
    b.bc_scr_free = b.n["st"]


def _emit_finish(nc, b):
    with nc.Block() as blk:
        @blk.gpsimd
        def _(g):
            g.wait_ge(b.s_st, b.n["st"])
            g.wait_ge(b.s_ld, b.n["ld"])


def kernel(L0_attn_x, L0_attn_w_qkv, L0_attn_w_proj, L0_attn_b_proj,
           L0_attn_w_dwc, L0_attn_b_dwc, L0_ffn_w1, L0_ffn_conv_w,
           L0_ffn_conv_b, L0_ffn_w2):
    import ml_dtypes
    x = np.ascontiguousarray(np.asarray(L0_attn_x, dtype=np.float32))
    shards = [np.ascontiguousarray(x[2 * i:2 * i + 2].reshape(NSEQ * SEQ, C))
              for i in range(N_CORES)]
    wq = np.ascontiguousarray(np.asarray(L0_attn_w_qkv, dtype=np.float32))
    wp = np.ascontiguousarray(np.asarray(L0_attn_w_proj, dtype=np.float32))
    bp = np.ascontiguousarray(np.asarray(L0_attn_b_proj, dtype=np.float32).reshape(1, C))
    wd = np.ascontiguousarray(np.asarray(L0_attn_w_dwc, dtype=np.float32))
    bd = np.ascontiguousarray(np.asarray(L0_attn_b_dwc, dtype=np.float32).reshape(C, 1))
    w1 = np.ascontiguousarray(np.asarray(L0_ffn_w1, dtype=np.float32))
    cw = np.ascontiguousarray(np.asarray(L0_ffn_conv_w, dtype=np.float32).reshape(HID, 3))
    cb = np.ascontiguousarray(np.asarray(L0_ffn_conv_b, dtype=np.float32).reshape(HID, 1))
    w2 = np.ascontiguousarray(np.asarray(L0_ffn_w2, dtype=np.float32))
    idf = np.eye(128, dtype=np.float32)
    idb = np.eye(128, dtype=ml_dtypes.bfloat16)
    cmask = np.zeros((2, 128), dtype=np.float32)
    cmask[0, 0:64] = 1.0
    cmask[1, 64:128] = 1.0

    nc = build(taps=_TAPS, maxphase=_MAXPHASE)
    in_maps = [{"x": shards[i], "w_qkv": wq, "w_proj": wp, "b_proj": bp,
                "w_dwc": wd, "b_dwc": bd, "w1": w1, "conv_w": cw, "conv_b": cb,
                "w2": w2, "idf": idf, "idb": idb, "cmask": cmask} for i in range(N_CORES)]
    res = run_bass_kernel_spmd(nc, in_maps, core_ids=list(range(N_CORES)))
    out = np.concatenate([np.asarray(r["y"]).reshape(NSEQ, SEQ, C)
                          for r in res.results], axis=0)
    return out.astype(np.float32)
